# version 02
# baseline (speedup 1.0000x reference)
import numpy as np
import ml_dtypes
import concourse.bass as bass
import concourse.mybir as mybir
from concourse.bass_utils import run_bass_kernel_spmd
from contextlib import ExitStack

F32 = mybir.dt.float32
BF16 = mybir.dt.bfloat16
AF = mybir.ActivationFunctionType
ALU = mybir.AluOpType

S, D, H, HD = 4096, 1024, 16, 64
TW = 512
NT = S // TW
NC = 8
CK = 3 * S // 4
EXPB = -8.0


def _band_masks():
    kr = np.arange(108)
    v = kr - 6
    jrel = 4 * (v // 3) + (v % 3) + 1
    qr = np.arange(128)
    m = (np.abs(qr[None, :] - jrel[:, None]) <= 8).astype(np.float32)
    first = m * (kr >= 6)[:, None]
    last = m * (kr < 102)[:, None]
    rep = np.zeros((128, H * 128), np.float32)
    rep[:108] = np.tile(m, (1, H))
    f = np.zeros((128, 128), np.float32)
    f[:108] = first
    l = np.zeros((128, 128), np.float32)
    l[:108] = last
    return (rep.astype(ml_dtypes.bfloat16), f.astype(ml_dtypes.bfloat16),
            l.astype(ml_dtypes.bfloat16))


class Prog:

    ENGINES = ("pe", "act", "dve", "sp", "gp")

    def __init__(self, nc, es):
        self.nc = nc
        self.es = es
        self.ops = {e: [] for e in self.ENGINES}
        self.sem = {e: es.enter_context(nc.semaphore(f"sem_{e}"))
                    for e in ("pe", "act", "dve")}
        self.dma_sems = {}
        self.bufs = {}
        self.milestones = {e: set() for e in ("pe", "act", "dve")}
        self.final_slots = []

    def _buf(self, name):
        return self.bufs.setdefault(name, {"w": None, "r": {}})

    def _addwait(self, eng, key, val, waits):
        if key == eng and key in self.sem:
            pass
        waits.append((key, val))
        if key in self.milestones:
            self.milestones[key].add(val)

    def op(self, eng, fn, reads=(), writes=(), dma=None):
        waits = []

        def need(k):
            return not (dma is None and eng == "pe" and k == "pe")

        for b in reads:
            st = self._buf(b)
            if st["w"] is not None and need(st["w"][0]):
                self._addwait(eng, *st["w"], waits)
        for b in writes:
            st = self._buf(b)
            if st["w"] is not None and need(st["w"][0]):
                self._addwait(eng, *st["w"], waits)
            for k, v in st["r"].items():
                if need(k):
                    self._addwait(eng, k, v, waits)

        if dma is not None:
            if dma not in self.dma_sems:
                self.dma_sems[dma] = [self.es.enter_context(
                    self.nc.semaphore(f"ds_{dma}")), 0]
            self.dma_sems[dma][1] += 16
            key, val = dma, self.dma_sems[dma][1]
        else:
            key, val = eng, len(self.ops[eng])

        for b in reads:
            self._buf(b)["r"][key] = val
        for b in writes:
            st = self._buf(b)
            st["w"] = (key, val)
            st["r"] = {}

        self.ops[eng].append((fn, waits, dma))

    def barrier(self):
        targets = [(e, len(self.ops[e]) - 1) for e in ("pe", "act", "dve")
                   if self.ops[e]]
        targets += [(k, v[1]) for k, v in self.dma_sems.items()]
        for eng in self.ENGINES:
            waits = []
            for key, val in targets:
                if key != eng:
                    self._addwait(eng, key, val, waits)
            self.ops[eng].append((None, waits, None))

    def final_wait(self, eng, slots):
        waits = [(sl, self.dma_sems[sl][1]) for sl in slots]
        self.ops[eng].append((None, waits, None))

    def prepare(self):
        self.rank = {e: {idx: i + 1 for i, idx in enumerate(sorted(ms))}
                     for e, ms in self.milestones.items()}
        self.water = {e: {} for e in self.ENGINES}

    def emit_engine(self, eng, engobj):
        rank = self.rank
        water = self.water
        for opi, (fn, waits, dma) in enumerate(self.ops[eng]):
            for key, val in waits:
                if key in rank:
                    v = rank[key].get(val)
                    assert v is not None, (eng, key, val)
                    semh = self.sem[key]
                else:
                    v = val
                    semh = self.dma_sems[key][0]
                if water[eng].get(key, 0) >= v:
                    continue
                water[eng][key] = v
                engobj.wait_ge(semh, v)
            if fn is None:
                continue
            inst = fn(engobj)
            if dma is not None:
                inst.then_inc(self.dma_sems[dma][0], 16)
            elif eng in rank and opi in rank[eng]:
                inst.then_inc(self.sem[eng], 1)


def build_nc(stages="abc"):
    nc = bass.Bass("TRN2", target_bir_lowering=False)

    x = nc.dram_tensor("x", [S, D], F32, kind="ExternalInput").ap()
    q_w = nc.dram_tensor("q_w", [D, D], F32, kind="ExternalInput").ap()
    k_w = nc.dram_tensor("k_w", [D, D], F32, kind="ExternalInput").ap()
    v_w = nc.dram_tensor("v_w", [D, D], F32, kind="ExternalInput").ap()
    o_w = nc.dram_tensor("o_w", [D, D], F32, kind="ExternalInput").ap()
    o_b = nc.dram_tensor("o_b", [1, D], F32, kind="ExternalInput").ap()
    us = nc.dram_tensor("us", [1, 1], F32, kind="ExternalInput").ap()
    w_r = nc.dram_tensor("w_r", [D, D], F32, kind="ExternalInput").ap()
    w_k = nc.dram_tensor("w_k", [D, D], F32, kind="ExternalInput").ap()
    w_v = nc.dram_tensor("w_v", [D, D], F32, kind="ExternalInput").ap()
    mu_r = nc.dram_tensor("mu_r", [D], F32, kind="ExternalInput").ap()
    mu_k = nc.dram_tensor("mu_k", [D], F32, kind="ExternalInput").ap()
    ident_h = nc.dram_tensor("ident", [128, 128], BF16, kind="ExternalInput").ap()
    maskr_h = nc.dram_tensor("maskr", [128, H * 128], BF16, kind="ExternalInput").ap()
    maskf_h = nc.dram_tensor("maskf", [128, 128], BF16, kind="ExternalInput").ap()
    maskl_h = nc.dram_tensor("maskl", [128, 128], BF16, kind="ExternalInput").ap()
    out = nc.dram_tensor("out", [S, D], F32, kind="ExternalOutput").ap()

    xnT_scr = nc.dram_tensor("xnT_scr", [8, 128, S + 1], BF16, kind="Internal").ap()
    ktc_scr = nc.dram_tensor("ktc_scr", [8, 128, CK], BF16, kind="Internal").ap()
    v64_scr = nc.dram_tensor("v64_scr", [CK, D], BF16, kind="Internal").ap()
    wt_scr = nc.dram_tensor("wt_scr", [3, 128, 8 * D], BF16, kind="Internal").ap()

    es = ExitStack()

    def alloc(name, cols, dtype=BF16):
        return es.enter_context(nc.sbuf_tensor(name, [128, cols], dtype))

    ident = alloc("identt", 128)
    maskr = alloc("maskr_t", H * 128)
    maskf = alloc("maskf_t", 128)
    maskl = alloc("maskl_t", 128)
    biasst = alloc("biasst", 128)
    obrep = alloc("obrep", D)
    obrow = es.enter_context(nc.sbuf_tensor("obrow", [1, D], BF16))
    onesrow = es.enter_context(nc.sbuf_tensor("onesrow", [1, 128], BF16))
    onesf = es.enter_context(nc.sbuf_tensor("onesf", [1, 128], F32))
    muscal = alloc("muscal", 16, F32)
    qscale = alloc("qscale", 1, F32)
    cexpb = alloc("cexpb", 1, F32)
    ceps = alloc("ceps", 1, F32)
    zcell = alloc("zcell", 8)
    tinyv = es.enter_context(nc.sbuf_tensor("tinyv", [1, 8], F32))
    lnsml = alloc("lnsml", 24, F32)
    kts = alloc("kts", 8 * 1024)
    vs = alloc("vs", 8 * 2048)
    A1 = alloc("A1", 24576)
    A2 = alloc("A2", 16384)
    A3 = alloc("A3", 25600)

    nc.const_aps.register(F32, 1e-5, ceps[:, 0:1])

    ps_f = es.enter_context(nc.psum_tensor("ps_f", [128, 3072], F32))
    ps_b = es.enter_context(nc.psum_tensor("ps_b", [128, 2048], BF16))

    def fb(bank, cols=512):
        return ps_f[:, bank * 512: bank * 512 + cols]

    kw_sb = A1[:, 0:8192]
    vw_sb = A1[:, 8192:16384]
    wstage = A1[:, 16384:24576]
    xbuf = [A2[:, 0:8192].bitcast(F32), A2[:, 8192:16384].bitcast(F32)]
    xntp = A3[:, 0:4096]
    xnst = A3[:, 4096:8192]
    xnbf = A3[:, 8192:9216]
    ktcst = A3[:, 9216:12288]
    v64st = [A3[:, 12288:13312], A3[:, 13312:14336]]
    wtevac = A3[:, 14336:15360]
    qw_sb = A2[:, 0:8192]
    ow_sb = A2[:, 8192:16384]
    bandk = A1[:, 0:3328]
    bandvi = [A1[:, 4096 + i * 2048: 4096 + (i + 1) * 2048] for i in range(4)]
    bandpm = [A1[:, 12288 + i * 2048: 12288 + (i + 1) * 2048] for i in range(4)]
    bandp = A1[:, 20480:22528]
    bandvp = A1[:, 22528:23552]
    xnt_b = A3[:, 0:4096]
    qte = A3[:, 4096:8192]
    qto = A3[:, 8192:12288]
    attnt = A3[:, 12288:16384]
    ptil = [A3[:, 16384 + i * 1024: 16384 + (i + 1) * 1024] for i in range(2)]
    recb = A3[:, 18432:20480].bitcast(F32)
    lnb = A3[:, 20480:22528].bitcast(F32)
    outst = A3[:, 22528:24576].bitcast(F32)
    wrt = A1[:, 0:8192]
    wkt = A1[:, 8192:16384]
    wvt = A1[:, 16384:24576]
    xnt_c = A3[:, 0:4104]
    xr_c = A3[:, 4104:8200]
    xk_c = A3[:, 8200:12296]
    kact = A3[:, 12296:16392]
    dtmp = A3[:, 16392:16904]
    krel = A3[:, 16904:17416]
    rt_c = A3[:, 17416:18440].bitcast(F32)
    outc = A3[:, 18440:22536].bitcast(F32)

    p = Prog(nc, es)
    OP = p.op

    OP("sp", lambda e: e.dma_start(out=ident[:], in_=ident_h), writes=["ident"], dma="ident")
    OP("sp", lambda e: e.dma_start(out=maskr[:], in_=maskr_h), writes=["maskr"], dma="maskr")
    OP("sp", lambda e: e.dma_start(out=maskf[:], in_=maskf_h), writes=["maskf"], dma="maskf")
    OP("sp", lambda e: e.dma_start(out=maskl[:], in_=maskl_h), writes=["maskl"], dma="maskl")
    OP("sp", lambda e: e.dma_start(out=muscal[:, 0:8], in_=mu_r.rearrange("(i p) -> p i", p=128),
                                   allow_slow_non_contiguous=True),
       writes=["mu"], dma="mu")
    OP("sp", lambda e: e.dma_start(out=muscal[:, 8:16], in_=mu_k.rearrange("(i p) -> p i", p=128),
                                   allow_slow_non_contiguous=True),
       writes=["mu2"], dma="mu2")
    OP("sp", lambda e: e.dma_start(out=tinyv[0:1, 0:1], in_=us), writes=["usv"], dma="usv")
    OP("gp", lambda e: e.dma_start(out=kw_sb.rearrange("p (i m) -> p i m", i=8),
                                   in_=k_w.rearrange("(i p) m -> p i m", p=128)),
       writes=["kw"], dma="kw")
    OP("gp", lambda e: e.dma_start(out=vw_sb.rearrange("p (i m) -> p i m", i=8),
                                   in_=v_w.rearrange("(i p) m -> p i m", p=128)),
       writes=["vw"], dma="vw")
    OP("gp", lambda e: e.dma_start(out=obrow[:], in_=o_b), writes=["obrow"], dma="obrow")

    OP("dve", lambda e: e.memset(ceps[:, 0:1], 1e-5), writes=["ceps"])
    OP("dve", lambda e: e.memset(zcell[:], 0.0), writes=["zcell"])
    OP("dve", lambda e: e.memset(onesrow[:], 1.0), writes=["onesrow"])
    OP("dve", lambda e: e.memset(biasst[:], 1.0 / 128.0), writes=["biasst"])
    OP("dve", lambda e: e.memset(
        vs[:].rearrange("p (k h two s) -> p k h two s", k=8, h=H, two=2)[:, :, :, 1, :], 1.0),
        writes=["vs"])
    OP("act", lambda e: e.activation(tinyv[0:1, 1:2], tinyv[0:1, 0:1], AF.Exp, scale=-1.0),
       reads=["usv"], writes=["usv1"])
    OP("dve", lambda e: e.tensor_scalar(tinyv[0:1, 2:3], tinyv[0:1, 1:2], 1.0, None, ALU.add),
       reads=["usv1"], writes=["usv2"])
    OP("dve", lambda e: e.reciprocal(tinyv[0:1, 3:4], tinyv[0:1, 2:3]),
       reads=["usv2"], writes=["usv3"])
    OP("dve", lambda e: e.tensor_scalar(tinyv[0:1, 4:5], tinyv[0:1, 3:4], 0.125, None, ALU.mult),
       reads=["usv3"], writes=["usv4"])
    OP("dve", lambda e: e.memset(onesf[:], 1.0), writes=["onesf"])
    OP("pe", lambda e: e.matmul(fb(1, 1), onesf[:], tinyv[0:1, 4:5], start=True, stop=True),
       reads=["usv4", "onesf"], writes=["bank1"])
    OP("dve", lambda e: e.tensor_copy(qscale[:, 0:1], fb(1, 1)),
       reads=["bank1"], writes=["qscale"])
    OP("gp", lambda e: e.dma_start(out=xnT_scr[:, :, 0:1].rearrange("i p one -> p (i one)"),
                                   in_=zcell[:], allow_slow_non_contiguous=True),
       reads=["zcell"], dma="zscr")
    for half in range(2):
        OP("pe", lambda e, h=half: e.matmul(fb(0), onesrow[:],
                                            obrow[0:1, h * 512:(h + 1) * 512],
                                            start=True, stop=True),
           reads=["obrow", "onesrow"], writes=["bank0"])
        OP("dve", lambda e, h=half: e.tensor_copy(obrep[:, h * 512:(h + 1) * 512], fb(0)),
           reads=["bank0"], writes=["obrep"])

    for wi, hbm in enumerate((w_r, w_k, w_v)):
        OP("gp", lambda e, hbm=hbm: e.dma_start(
            out=wstage.rearrange("p (i m) -> p i m", i=8),
            in_=hbm.rearrange("(i p) m -> p i m", p=128)), writes=["wstage"], dma="wstage")
        for i in range(8):
            for m in range(8):
                OP("pe", lambda e, i=i, m=m: e.transpose(
                    ps_b[:, m * 128:(m + 1) * 128],
                    wstage[:, m * 1024 + i * 128: m * 1024 + (i + 1) * 128], ident[:]),
                    reads=["wstage", "ident"], writes=["bank6"])
            if wi == 2:
                OP("dve", lambda e: e.tensor_scalar(wtevac[:], ps_b[:, 0:1024], 0.5,
                                                    None, ALU.mult),
                   reads=["bank6"], writes=["wtevac"])
            else:
                OP("dve", lambda e: e.tensor_copy(wtevac[:], ps_b[:, 0:1024]),
                   reads=["bank6"], writes=["wtevac"])
            OP("gp", lambda e, wi=wi, i=i: e.dma_start(
                out=wt_scr[wi, :, i * 1024:(i + 1) * 1024], in_=wtevac[:]),
                reads=["wtevac"], dma=f"wts{wi}")

    for t in range(NT):
        xb = xbuf[t % 2]
        xslot = f"xb{t % 2}"
        xg = x.rearrange("(c g f) d -> c g f d", g=32, f=4)
        og = out.rearrange("(c g f) d -> c g f d", g=32, f=4)
        for c4 in range(4):
            c = 4 * t + c4
            OP("sp", lambda e, c=c, c4=c4, xg=xg, xb=xb: e.dma_start(
                out=xb[0:32, c4 * 1024:(c4 + 1) * 1024],
                in_=xg[c, :, 0, :]), writes=[xslot], dma=xslot)
            OP("sp", lambda e, c=c, c4=c4, xg=xg, xb=xb: e.dma_start(
                out=xb[32:128, c4 * 1024:(c4 + 1) * 1024],
                in_=xg[c, :, 1:4, :]),
                writes=[xslot], dma=xslot)
            OP("gp", lambda e, c=c, c4=c4, og=og, xb=xb: e.dma_start(
                out=og[c, :, 0, :],
                in_=xb[0:32, c4 * 1024:(c4 + 1) * 1024]),
                reads=[xslot], writes=[f"outr{t}"], dma="pre")
            OP("gp", lambda e, c=c, c4=c4, og=og, xb=xb: e.dma_start(
                out=og[c, :, 1:4, :],
                in_=xb[32:128, c4 * 1024:(c4 + 1) * 1024]),
                reads=[xslot], writes=[f"outr{t}"], dma="pre")
        for c4 in range(4):
            c = 4 * t + c4
            xc = xb[:, c4 * 1024:(c4 + 1) * 1024]
            OP("dve", lambda e, xc=xc: e.bn_stats(lnsml[:, 0:6], xc[:, 0:512]),
               reads=[xslot], writes=["lns"])
            OP("dve", lambda e, xc=xc: e.bn_stats(lnsml[:, 6:12], xc[:, 512:1024]),
               reads=[xslot], writes=["lns2"])
            OP("dve", lambda e: e.bn_aggr(lnsml[:, 12:14], lnsml[:, 0:12]),
               reads=["lns", "lns2"], writes=["mv"])
            OP("act", lambda e: e.activation(lnsml[:, 14:15], lnsml[:, 13:14], AF.Ln,
                                             bias=1e-5), reads=["mv", "ceps"], writes=["lnv"])
            OP("act", lambda e: e.activation(lnsml[:, 15:16], lnsml[:, 14:15], AF.Exp,
                                             scale=-0.5), reads=["lnv"], writes=["rstd"])
            OP("dve", lambda e, xc=xc: e.tensor_scalar(
                xnbf[:], xc, lnsml[:, 12:13], lnsml[:, 15:16],
                ALU.subtract, ALU.mult),
               reads=[xslot, "mv", "rstd"], writes=["xnbf"])
            for j in range(8):
                OP("pe", lambda e, j=j: e.transpose(
                    ps_b[:, 1024 + j * 128: 1024 + (j + 1) * 128],
                    xnbf[:, j * 128:(j + 1) * 128], ident[:]),
                    reads=["xnbf", "ident"], writes=["bank7"])
            OP("dve", lambda e, c4=c4: e.tensor_copy(
                xntp[:].rearrange("p (i w) -> p i w", i=8)[:, :, c4 * 128:(c4 + 1) * 128],
                ps_b[:, 1024:2048].rearrange("p (i w) -> p i w", i=8)),
                reads=["bank7"], writes=["xntp"])
            OP("dve", lambda e, c4=c4: e.tensor_copy(
                xnst[:].rearrange("p (i w) -> p i w", i=8)[:, :, c4 * 128:(c4 + 1) * 128]
                    .rearrange("p i (g o) -> p i g o", o=4)[:, :, :, 0:1],
                ps_b[:, 1024:2048].rearrange("p (i w) -> p i w", i=8)[:, :, 0:32]
                    .rearrange("p i (g o) -> p i g o", o=1)),
                reads=["bank7"], writes=["xnst"])
            OP("dve", lambda e, c4=c4: e.tensor_copy(
                xnst[:].rearrange("p (i w) -> p i w", i=8)[:, :, c4 * 128:(c4 + 1) * 128]
                    .rearrange("p i (g o) -> p i g o", o=4)[:, :, :, 1:4],
                ps_b[:, 1024:2048].rearrange("p (i w) -> p i w", i=8)[:, :, 32:128]
                    .rearrange("p i (g o) -> p i g o", o=3)),
                reads=["bank7"], writes=["xnst"])
        OP("gp", lambda e, t=t: e.dma_start(
            out=xnT_scr[:, :, t * 512 + 1: (t + 1) * 512 + 1].rearrange("i p w -> p i w"),
            in_=xnst[:].rearrange("p (i w) -> p i w", i=8)),
            reads=["xnst"], dma="xnts")
        for m in range(8):
            pb = f"bank{m % 2}"
            for i in range(8):
                OP("pe", lambda e, m=m, i=i: e.matmul(
                    fb(m % 2), kw_sb[:, i * 1024 + m * 128: i * 1024 + (m + 1) * 128],
                    xntp[:, i * 512:(i + 1) * 512], start=(i == 0), stop=(i == 7)),
                    reads=["kw", "xntp"], writes=[pb])
            OP("dve", lambda e, m=m, pb=pb, t=t: e.tensor_copy(
                kts[:, m * 1024 + t * 128: m * 1024 + (t + 1) * 128]
                    .rearrange("p (s w) -> p s w", s=4),
                fb(m % 2).rearrange("p (s w) -> p s w", s=4)[:, :, 0:32]),
                reads=[f"bank{m % 2}"], writes=["kts"])
            OP("dve", lambda e, m=m, pb=pb: e.tensor_copy(
                ktcst[:, m * 384:(m + 1) * 384].rearrange("p (s w) -> p s w", s=4),
                fb(m % 2).rearrange("p (s w) -> p s w", s=4)[:, :, 32:128]),
                reads=[f"bank{m % 2}"], writes=["ktcst"])
        OP("gp", lambda e, t=t: e.dma_start(
            out=ktc_scr[:, :, t * 384:(t + 1) * 384].rearrange("i p w -> p i w"),
            in_=ktcst[:].rearrange("p (i w) -> p i w", i=8)),
            reads=["ktcst"], dma="ktcs")
        for c4 in range(4):
            cg = 4 * t + c4
            pb = 2 + 2 * (c4 % 2)
            for half in range(2):
                for i in range(8):
                    OP("pe", lambda e, c4=c4, half=half, i=i, pb=pb: e.matmul(
                        ps_f[:, pb * 512 + half * 512: pb * 512 + (half + 1) * 512],
                        xntp[:, i * 512 + c4 * 128: i * 512 + (c4 + 1) * 128],
                        vw_sb[:, i * 1024 + half * 512: i * 1024 + (half + 1) * 512],
                        start=(i == 0), stop=(i == 7)),
                        reads=["vw", "xntp"], writes=[f"bank{pb + half}"])
            OP("dve", lambda e, cg=cg, pb=pb: e.tensor_copy(
                vs[32 * (cg % 4): 32 * (cg % 4) + 32,
                   (cg // 4) * 2048: (cg // 4) * 2048 + 2048]
                    .rearrange("p (h two s) -> p h two s", h=H, two=2)[:, :, 0:1, :],
                ps_f[0:32, pb * 512: pb * 512 + 1024]
                    .rearrange("p (h two s) -> p h two s", two=1, s=64)),
                reads=[f"bank{pb}", f"bank{pb+1}"], writes=["vs"])
            OP("dve", lambda e, v64=v64st[c4 % 2], pb=pb: e.tensor_copy(
                v64[32:64, :], ps_f[32:64, pb * 512: pb * 512 + 1024]),
                reads=[f"bank{pb}", f"bank{pb+1}"], writes=[f"v64st{c4 % 2}"])
            OP("dve", lambda e, v64=v64st[c4 % 2], pb=pb: e.tensor_copy(
                v64[64:128, :], ps_f[64:128, pb * 512: pb * 512 + 1024]),
                reads=[f"bank{pb}", f"bank{pb+1}"], writes=[f"v64st{c4 % 2}"])
            OP("gp", lambda e, cg=cg, v64=v64st[c4 % 2]: e.dma_start(
                out=v64_scr[cg * 96:(cg + 1) * 96, :], in_=v64[32:128, :]),
                reads=[f"v64st{c4 % 2}"], dma=f"v64w{c4 % 2}")

    p.barrier()

    run_b = "b" in stages
    run_c = "c" in stages
    blevel = 4
    for ch in stages:
        if ch.isdigit():
            blevel = int(ch)
    import os as _os
    bandlv = int(_os.environ.get("BANDLV", "4"))
    if run_b:

        OP("gp", lambda e: e.dma_start(out=qw_sb.rearrange("p (i m) -> p i m", i=8),
                                       in_=q_w.rearrange("(i p) m -> p i m", p=128)),
           writes=["qw"], dma="qw")
        OP("gp", lambda e: e.dma_start(out=ow_sb.rearrange("p (i m) -> p i m", i=8),
                                       in_=o_w.rearrange("(i p) m -> p i m", p=128)),
           writes=["ow"], dma="ow")
        OP("dve", lambda e: e.memset(qte[64:128, :], 0.0), writes=["qte"])
        OP("dve", lambda e: e.memset(qto[0:64, :], 0.0), writes=["qto"])
        OP("dve", lambda e: e.memset(bandk[:], 0.0), writes=["bandk"])
        OP("dve", lambda e: e.memset(bandvp[:], 0.0), writes=["bandvp"])
        for sg in range(4):
            OP("dve", lambda e, sg=sg: e.memset(bandvi[sg][:], 0.0), writes=[f"bandvi{sg}"])
            OP("dve", lambda e, sg=sg: e.memset(
                bandvi[sg].rearrange("p (h two s) -> p h two s", h=H, two=2)[:, :, 1, :], 1.0),
                writes=[f"bandvi{sg}"])

        for t in range(NT):
            OP("sp", lambda e, t=t: e.dma_start(
                out=xnt_b[:].rearrange("p (i w) -> p i w", i=8),
                in_=xnT_scr[:, :, t * 512 + 1:(t + 1) * 512 + 1].rearrange("i p w -> p i w")),
                writes=["xnt_b"], dma="xnt_b")
            lo = 384 * t - 6
            s0, s1 = max(lo, 0), min(lo + 416, CK)
            OP("sp", lambda e, lo=lo, s0=s0, s1=s1: e.dma_start(
                out=bandk[:].rearrange("p (i w) -> p i w", i=8)[:, :, s0 - lo: s1 - lo],
                in_=ktc_scr[:, :, s0:s1].rearrange("i p w -> p i w")),
                writes=["bandk"], dma="bandk")
            if t == NT - 1:
                pass
            for m in range(8):
                pb = f"bank{m % 2}"
                for i in range(8):
                    OP("pe", lambda e, m=m, i=i: e.matmul(
                        fb(m % 2), qw_sb[:, i * 1024 + m * 128: i * 1024 + (m + 1) * 128],
                        xnt_b[:, i * 512:(i + 1) * 512], start=(i == 0), stop=(i == 7)),
                        reads=["qw", "xnt_b"], writes=[pb])
                OP("dve", lambda e, m=m: e.tensor_copy(
                    qte[0:64, m * 512:(m + 1) * 512], fb(m % 2)[0:64, :]),
                   reads=[pb], writes=["qte"])
                OP("dve", lambda e, m=m: e.tensor_copy(
                    qto[64:128, m * 512:(m + 1) * 512], fb(m % 2)[64:128, :]),
                   reads=[pb], writes=["qto"])
            for sg in range(4 if blevel >= 2 else 0):
                sglob = 4 * t + sg
                vlo = 96 * sglob - 6
                v0, v1 = max(vlo, 0), min(vlo + 128, CK)
                OP("sp", lambda e, vlo=vlo, v0=v0, v1=v1: e.dma_start(
                    out=bandvp[v0 - vlo: v0 - vlo + (v1 - v0), :], in_=v64_scr[v0:v1, :]),
                    writes=["bandvp"], dma="bandvp")
                OP("dve", lambda e, sg=sg: e.tensor_copy(
                    bandvi[sg].rearrange("p (h two s) -> p h two s", h=H, two=2)[:, :, 0:1, :],
                    bandvp[:].rearrange("p (h two s) -> p h two s", two=1, s=64)),
                    reads=["bandvp"], writes=[f"bandvi{sg}"])
                for halfh in range(2):
                    pb2 = 2 * halfh
                    for hh in range(8 if bandlv >= 2 else 0):
                        h = halfh * 8 + hh
                        qsrc = qte if h % 2 == 0 else qto
                        qname = "qte" if h % 2 == 0 else "qto"
                        OP("pe", lambda e, h=h, hh=hh, sg=sg, pb2=pb2, qsrc=qsrc: e.matmul(
                            ps_f[:, pb2 * 512 + hh * 128: pb2 * 512 + (hh + 1) * 128],
                            bandk[:, (h // 2) * 416 + 96 * sg:
                                  (h // 2) * 416 + 96 * sg + 128],
                            qsrc[:, (h // 2) * 512 + sg * 128:
                                 (h // 2) * 512 + (sg + 1) * 128],
                            start=True, stop=True),
                            reads=["bandk", qname], writes=[f"bank{pb2 + hh // 4}"])
                    if bandlv >= 3:
                        OP("act", lambda e, halfh=halfh, pb2=pb2: e.activation(
                            bandp[:, halfh * 1024:(halfh + 1) * 1024],
                            ps_f[:, pb2 * 512: pb2 * 512 + 1024], AF.Exp,
                            scale=qscale[:, 0:1]),
                            reads=[f"bank{pb2}", f"bank{pb2 + 1}", "qscale"],
                            writes=["bandp"])
                if bandlv < 4:
                    pass
                elif sglob == 0 or sglob == NT * 4 - 1:
                    mk = maskf if sglob == 0 else maskl
                    mname = "maskf" if sglob == 0 else "maskl"
                    for h in range(H):
                        OP("dve", lambda e, sg=sg, h=h, mk=mk: e.tensor_mul(
                            bandpm[sg][:, h * 128:(h + 1) * 128],
                            bandp[:, h * 128:(h + 1) * 128], mk[:]),
                            reads=["bandp", mname], writes=[f"bandpm{sg}"])
                else:
                    OP("dve", lambda e, sg=sg: e.tensor_mul(bandpm[sg][:], bandp[:], maskr[:]),
                       reads=["bandp", "maskr"], writes=[f"bandpm{sg}"])
            for pr in range(8 if blevel >= 3 else 0):
                he, ho = 2 * pr, 2 * pr + 1
                for g in range(8):
                    pb2 = 2 * (g % 2)
                    for (par, tp_c, kk) in ((0, 0, 2 * g), (0, 64, 2 * g + 1),
                                            (1, 0, 2 * g), (1, 64, 2 * g + 1)):
                        qsrc = qte if par == 0 else qto
                        qname = "qte" if par == 0 else "qto"
                        OP("pe", lambda e, par=par, tp_c=tp_c, kk=kk, pr=pr, pb2=pb2,
                            qsrc=qsrc: e.matmul(
                            ps_f[tp_c:tp_c + 64, pb2 * 512 + par * 512:
                                 pb2 * 512 + par * 512 + 512],
                            kts[:, pr * 1024 + kk * 64: pr * 1024 + (kk + 1) * 64],
                            qsrc[:, pr * 512:(pr + 1) * 512],
                            start=True, stop=True, tile_position=(0, tp_c)),
                            reads=["kts", qname], writes=[f"bank{pb2 + par}"])
                    OP("act", lambda e, g=g, pb2=pb2: e.activation(
                        ptil[g % 2][:], ps_f[:, pb2 * 512: pb2 * 512 + 1024], AF.Exp,
                        scale=qscale[:, 0:1]),
                        reads=[f"bank{pb2}", f"bank{pb2 + 1}", "qscale"],
                        writes=[f"ptil{g % 2}"])
                    for sub, col in ((0, 0), (1, 512)):
                        OP("pe", lambda e, g=g, pr=pr, sub=sub, col=col: e.matmul(
                            fb(4 + sub), vs[:, g * 2048 + (2 * pr + sub) * 128:
                                            g * 2048 + (2 * pr + sub) * 128 + 128],
                            ptil[g % 2][:, col:col + 512],
                            start=(g == 0), stop=False),
                            reads=[f"ptil{g % 2}", "vs"], writes=[f"bank{4 + sub}"])
                for sg in range(4 if blevel >= 2 else 0):
                    for sub in range(2):
                        h = 2 * pr + sub
                        OP("pe", lambda e, sg=sg, h=h, sub=sub, last=(sg == 3): e.matmul(
                            ps_f[:, (4 + sub) * 512 + sg * 128: (4 + sub) * 512 + (sg + 1) * 128],
                            bandvi[sg][:, h * 128:(h + 1) * 128],
                            bandpm[sg][:, h * 128:(h + 1) * 128],
                            start=False, stop=last),
                            reads=[f"bandvi{sg}", f"bandpm{sg}"], writes=[f"bank{4 + sub}"])
                OP("act", lambda e: e.activation(lnb[0:64, :], ps_f[64:128, 2048:3072], AF.Ln),
                   reads=["bank4", "bank5"], writes=["lnb"])
                OP("act", lambda e: e.activation(recb[0:64, :], lnb[0:64, :], AF.Exp, scale=-1.0),
                   reads=["lnb"], writes=["recb"])
                OP("dve", lambda e, pr=pr: e.tensor_mul(
                    attnt[0:64, pr * 512:(pr + 1) * 512], ps_f[0:64, 2048:2560],
                    recb[0:64, 0:512]),
                    reads=["bank4", "recb"], writes=["attnt"])
                OP("dve", lambda e, pr=pr: e.tensor_mul(
                    attnt[64:128, pr * 512:(pr + 1) * 512], ps_f[0:64, 2560:3072],
                    recb[0:64, 512:1024]),
                    reads=["bank5", "recb"], writes=["attnt"])
            for sub in range(4 if blevel >= 4 and blevel >= 3 else 0):
                for half in range(2):
                    pb = half
                    pname = f"bank{half}"
                    OP("pe", lambda e, half=half, pb=pb: e.matmul(
                        fb(pb), biasst[:], obrep[:, half * 512:(half + 1) * 512],
                        start=True, stop=False),
                        reads=["biasst", "obrep"], writes=[pname])
                    for i in range(8):
                        OP("pe", lambda e, sub=sub, half=half, i=i, pb=pb: e.matmul(
                            fb(pb), attnt[:, i * 512 + sub * 128: i * 512 + sub * 128 + 128],
                            ow_sb[:, i * 1024 + half * 512: i * 1024 + (half + 1) * 512],
                            start=False, stop=(i == 7)),
                            reads=["attnt", "ow"], writes=[pname])
                    OP("dve", lambda e, half=half, pb=pb: e.tensor_copy(
                        outst[:, half * 512:(half + 1) * 512], fb(pb)),
                        reads=[pname], writes=["outst"])
                OP("gp", lambda e, t=t, sub=sub: e.dma_start(
                    out=out[t * 512 + sub * 128: t * 512 + (sub + 1) * 128, :],
                    in_=outst[:, 0:1024], accum_op=ALU.add),
                    reads=["outst"], writes=[f"outr{t}"], dma="oacc")

    p.barrier()

    if run_c:
        for wi, dst in enumerate((wrt, wkt, wvt)):
            OP("sp", lambda e, wi=wi, dst=dst: e.dma_start(
                out=dst.rearrange("p (i m) -> p i m", i=8),
                in_=wt_scr[wi].rearrange("p (i m) -> p i m", i=8)),
                writes=[f"wt{wi}"], dma=f"wt{wi}")
        for t in range(NT):
            OP("sp", lambda e, t=t: e.dma_start(
                out=xnt_c[:].rearrange("p (i w) -> p i w", i=8),
                in_=xnT_scr[:, :, t * 512:(t + 1) * 512 + 1].rearrange("i p w -> p i w")),
                writes=["xnt_c"], dma="xnt_c")
            for i in range(8):
                cur = xnt_c[:, i * 513 + 1: i * 513 + 513]
                prev = xnt_c[:, i * 513: i * 513 + 512]
                OP("dve", lambda e, cur=cur, prev=prev: e.tensor_sub(dtmp[:], cur, prev),
                   reads=["xnt_c"], writes=["dtmp"])
                OP("dve", lambda e, i=i, prev=prev: e.scalar_tensor_tensor(
                    xr_c[:, i * 512:(i + 1) * 512], dtmp[:], muscal[:, i:i + 1], prev,
                    ALU.mult, ALU.add),
                    reads=["dtmp", "mu", "xnt_c"], writes=["xr"])
                OP("dve", lambda e, i=i, prev=prev: e.scalar_tensor_tensor(
                    xk_c[:, i * 512:(i + 1) * 512], dtmp[:], muscal[:, 8 + i:9 + i], prev,
                    ALU.mult, ALU.add),
                    reads=["dtmp", "mu2", "xnt_c"], writes=["xk"])
            for m in range(8):
                pname = f"bank{m % 2}"
                for i in range(8):
                    OP("pe", lambda e, m=m, i=i: e.matmul(
                        fb(m % 2), wkt[:, i * 1024 + m * 128: i * 1024 + (m + 1) * 128],
                        xk_c[:, i * 512:(i + 1) * 512], start=(i == 0), stop=(i == 7)),
                        reads=["wt1", "xk"], writes=[pname])
                OP("dve", lambda e, m=m: e.tensor_scalar(krel[:], fb(m % 2), 0.0, None, ALU.max),
                   reads=[pname], writes=["krel"])
                OP("dve", lambda e, m=m: e.tensor_mul(kact[:, m * 512:(m + 1) * 512],
                                                      krel[:], krel[:]),
                   reads=["krel"], writes=["kact"])
            for sub in range(4):
                for half in range(2):
                    pr_ = f"bank{2 + half}"
                    pv_ = f"bank{4 + half}"
                    for i in range(8):
                        OP("pe", lambda e, sub=sub, half=half, i=i: e.matmul(
                            fb(2 + half), xr_c[:, i * 512 + sub * 128: i * 512 + sub * 128 + 128],
                            wrt[:, i * 1024 + half * 512: i * 1024 + (half + 1) * 512],
                            start=(i == 0), stop=(i == 7)),
                            reads=["wt0", "xr"], writes=[pr_])
                    OP("act", lambda e, half=half: e.activation(rt_c[:], fb(2 + half), AF.Tanh,
                                                                scale=0.5),
                       reads=[pr_], writes=["rt"])
                    for i in range(8):
                        OP("pe", lambda e, sub=sub, half=half, i=i: e.matmul(
                            fb(4 + half), kact[:, i * 512 + sub * 128: i * 512 + sub * 128 + 128],
                            wvt[:, i * 1024 + half * 512: i * 1024 + (half + 1) * 512],
                            start=(i == 0), stop=(i == 7)),
                            reads=["wt2", "kact"], writes=[pv_])
                    OP("dve", lambda e, half=half: e.scalar_tensor_tensor(
                        outc[:, half * 512:(half + 1) * 512], rt_c[:], 1.0, fb(4 + half),
                        ALU.add, ALU.mult),
                        reads=["rt", pv_], writes=["outc"])
                OP("gp", lambda e, t=t, sub=sub: e.dma_start(
                    out=out[t * 512 + sub * 128: t * 512 + (sub + 1) * 128, :],
                    in_=outc[:, 0:1024], accum_op=ALU.add),
                    reads=["outc"], writes=[f"outr{t}"], dma="oaccC")

    slots = ["pre"] + (["oacc"] if run_b and blevel >= 4 else []) + (["oaccC"] if run_c else [])
    p.final_wait("gp", slots)

    p.prepare()
    with nc.Block() as block:
        @block.sync
        def _(eng):
            p.emit_engine("sp", eng)

        @block.gpsimd
        def _(eng):
            p.emit_engine("gp", eng)

        @block.vector
        def _(eng):
            p.emit_engine("dve", eng)

        @block.scalar
        def _(eng):
            p.emit_engine("act", eng)

        @block.tensor
        def _(eng):
            p.emit_engine("pe", eng)

    es.close()
    return nc


def make_inputs(inp):
    f32 = np.float32
    maskr, maskf, maskl = _band_masks()
    d = {
        "q_w": np.ascontiguousarray(inp["L0_attn_q_w"], f32),
        "k_w": np.ascontiguousarray(inp["L0_attn_k_w"], f32),
        "v_w": np.ascontiguousarray(inp["L0_attn_v_w"], f32),
        "o_w": np.ascontiguousarray(inp["L0_attn_o_w"], f32),
        "o_b": np.ascontiguousarray(inp["L0_attn_o_b"], f32).reshape(1, D),
        "us": np.asarray(inp["L0_attn_unity_scale"], f32).reshape(1, 1),
        "w_r": np.ascontiguousarray(inp["L0_ffn_W_r"], f32),
        "w_k": np.ascontiguousarray(inp["L0_ffn_W_k"], f32),
        "w_v": np.ascontiguousarray(inp["L0_ffn_W_v"], f32),
        "mu_r": np.ascontiguousarray(inp["L0_ffn_mu_r"], f32),
        "mu_k": np.ascontiguousarray(inp["L0_ffn_mu_k"], f32),
        "ident": np.eye(128, dtype=ml_dtypes.bfloat16),
        "maskr": maskr, "maskf": maskf, "maskl": maskl,
    }
    return d


_NC_CACHE = [None]


def kernel(**inputs):
    import os
    if _NC_CACHE[0] is None:
        _NC_CACHE[0] = build_nc(os.environ.get("KSTAGES", "abc"))
    nc = _NC_CACHE[0]
    base = make_inputs(inputs)
    xfull = np.ascontiguousarray(inputs["L0_attn_x"], np.float32)
    in_maps = []
    for c in range(NC):
        m = dict(base)
        m["x"] = xfull[c]
        in_maps.append(m)
    res = run_bass_kernel_spmd(nc, in_maps, core_ids=list(range(NC)))
    return np.stack([r["out"] for r in res.results], axis=0)


if __name__ == "__main__":
    rng = np.random.default_rng(0)
    fake = {
        "L0_attn_x": rng.standard_normal((NC, S, D), dtype=np.float32),
        "L0_attn_q_w": rng.standard_normal((D, D), dtype=np.float32) / 32,
        "L0_attn_k_w": rng.standard_normal((D, D), dtype=np.float32) / 32,
        "L0_attn_v_w": rng.standard_normal((D, D), dtype=np.float32) / 32,
        "L0_attn_o_w": rng.standard_normal((D, D), dtype=np.float32) / 32,
        "L0_attn_o_b": rng.standard_normal(D).astype(np.float32) * 0.01,
        "L0_attn_unity_scale": np.float32(1.618),
        "L0_ffn_W_r": (rng.random((D, D), dtype=np.float32) - 0.5) / 16,
        "L0_ffn_W_k": (rng.random((D, D), dtype=np.float32) - 0.5) / 16,
        "L0_ffn_W_v": (rng.random((D, D), dtype=np.float32) - 0.5) / 16,
        "L0_ffn_mu_r": rng.random(D, dtype=np.float32),
        "L0_ffn_mu_k": rng.random(D, dtype=np.float32),
    }
    out = kernel(**fake)
    print("ran", out.shape, out.dtype)


# version 03
# speedup vs baseline: 1.1520x; 1.1520x over previous
import numpy as np
import ml_dtypes
import concourse.bass as bass
import concourse.mybir as mybir
from concourse.bass_utils import run_bass_kernel_spmd
from contextlib import ExitStack

F32 = mybir.dt.float32
BF16 = mybir.dt.bfloat16
AF = mybir.ActivationFunctionType
ALU = mybir.AluOpType

S, D, H, HD = 4096, 1024, 16, 64
TW = 512
NT = S // TW
NC = 8
CK = 3 * S // 4
EXPB = -8.0


def _band_masks():
    kr = np.arange(108)
    v = kr - 6
    jrel = 4 * (v // 3) + (v % 3) + 1
    qr = np.arange(128)
    m = (np.abs(qr[None, :] - jrel[:, None]) <= 8).astype(np.float32)
    first = m * (kr >= 6)[:, None]
    last = m * (kr < 102)[:, None]
    rep = np.zeros((128, H * 128), np.float32)
    rep[:108] = np.tile(m, (1, H))
    f = np.zeros((128, 128), np.float32)
    f[:108] = first
    l = np.zeros((128, 128), np.float32)
    l[:108] = last
    return (rep.astype(ml_dtypes.bfloat16), f.astype(ml_dtypes.bfloat16),
            l.astype(ml_dtypes.bfloat16))


class Prog:

    ENGINES = ("pe", "act", "dve", "sp", "gp")

    def __init__(self, nc, es):
        self.nc = nc
        self.es = es
        self.ops = {e: [] for e in self.ENGINES}
        self.sem = {e: es.enter_context(nc.semaphore(f"sem_{e}"))
                    for e in ("pe", "act", "dve")}
        self.dma_sems = {}
        self.bufs = {}
        self.milestones = {e: set() for e in ("pe", "act", "dve")}
        self.final_slots = []

    def _buf(self, name):
        return self.bufs.setdefault(name, {"w": None, "r": {}})

    def _addwait(self, eng, key, val, waits):
        if key == eng and key in self.sem:
            pass
        waits.append((key, val))
        if key in self.milestones:
            self.milestones[key].add(val)

    def op(self, eng, fn, reads=(), writes=(), dma=None):
        waits = []

        def need(k):
            return not (dma is None and eng == "pe" and k == "pe")

        for b in reads:
            st = self._buf(b)
            if st["w"] is not None and need(st["w"][0]):
                self._addwait(eng, *st["w"], waits)
        for b in writes:
            st = self._buf(b)
            if st["w"] is not None and need(st["w"][0]):
                self._addwait(eng, *st["w"], waits)
            for k, v in st["r"].items():
                if need(k):
                    self._addwait(eng, k, v, waits)

        if dma is not None:
            if dma not in self.dma_sems:
                self.dma_sems[dma] = [self.es.enter_context(
                    self.nc.semaphore(f"ds_{dma}")), 0]
            self.dma_sems[dma][1] += 16
            key, val = dma, self.dma_sems[dma][1]
        else:
            key, val = eng, len(self.ops[eng])

        for b in reads:
            self._buf(b)["r"][key] = val
        for b in writes:
            st = self._buf(b)
            st["w"] = (key, val)
            st["r"] = {}

        self.ops[eng].append((fn, waits, dma))

    def barrier(self):
        targets = [(e, len(self.ops[e]) - 1) for e in ("pe", "act", "dve")
                   if self.ops[e]]
        targets += [(k, v[1]) for k, v in self.dma_sems.items()]
        for eng in self.ENGINES:
            waits = []
            for key, val in targets:
                if key != eng:
                    self._addwait(eng, key, val, waits)
            self.ops[eng].append((None, waits, None))

    def final_wait(self, eng, slots):
        waits = [(sl, self.dma_sems[sl][1]) for sl in slots]
        self.ops[eng].append((None, waits, None))

    def prepare(self):
        self.rank = {e: {idx: i + 1 for i, idx in enumerate(sorted(ms))}
                     for e, ms in self.milestones.items()}
        self.water = {e: {} for e in self.ENGINES}

    def emit_engine(self, eng, engobj):
        rank = self.rank
        water = self.water
        for opi, (fn, waits, dma) in enumerate(self.ops[eng]):
            for key, val in waits:
                if key in rank:
                    v = rank[key].get(val)
                    assert v is not None, (eng, key, val)
                    semh = self.sem[key]
                else:
                    v = val
                    semh = self.dma_sems[key][0]
                if water[eng].get(key, 0) >= v:
                    continue
                water[eng][key] = v
                engobj.wait_ge(semh, v)
            if fn is None:
                continue
            inst = fn(engobj)
            if dma is not None:
                inst.then_inc(self.dma_sems[dma][0], 16)
            elif eng in rank and opi in rank[eng]:
                inst.then_inc(self.sem[eng], 1)


def build_nc(stages="abc"):
    nc = bass.Bass("TRN2", target_bir_lowering=False)

    x = nc.dram_tensor("x", [S, D], F32, kind="ExternalInput").ap()
    q_w = nc.dram_tensor("q_w", [D, D], F32, kind="ExternalInput").ap()
    k_w = nc.dram_tensor("k_w", [D, D], F32, kind="ExternalInput").ap()
    v_w = nc.dram_tensor("v_w", [D, D], F32, kind="ExternalInput").ap()
    o_w = nc.dram_tensor("o_w", [D, D], F32, kind="ExternalInput").ap()
    o_b = nc.dram_tensor("o_b", [1, D], F32, kind="ExternalInput").ap()
    us = nc.dram_tensor("us", [1, 1], F32, kind="ExternalInput").ap()
    w_r = nc.dram_tensor("w_r", [D, D], F32, kind="ExternalInput").ap()
    w_k = nc.dram_tensor("w_k", [D, D], F32, kind="ExternalInput").ap()
    w_v = nc.dram_tensor("w_v", [D, D], F32, kind="ExternalInput").ap()
    mu_r = nc.dram_tensor("mu_r", [D], F32, kind="ExternalInput").ap()
    mu_k = nc.dram_tensor("mu_k", [D], F32, kind="ExternalInput").ap()
    ident_h = nc.dram_tensor("ident", [128, 128], BF16, kind="ExternalInput").ap()
    maskr_h = nc.dram_tensor("maskr", [128, H * 128], BF16, kind="ExternalInput").ap()
    maskf_h = nc.dram_tensor("maskf", [128, 128], BF16, kind="ExternalInput").ap()
    maskl_h = nc.dram_tensor("maskl", [128, 128], BF16, kind="ExternalInput").ap()
    out = nc.dram_tensor("out", [S, D], F32, kind="ExternalOutput").ap()

    xnT_scr = nc.dram_tensor("xnT_scr", [8, 128, S + 1], BF16, kind="Internal").ap()
    ktc_scr = nc.dram_tensor("ktc_scr", [8, 128, CK], BF16, kind="Internal").ap()
    v64_scr = nc.dram_tensor("v64_scr", [CK, D], BF16, kind="Internal").ap()
    wt_scr = nc.dram_tensor("wt_scr", [3, 128, 8 * D], BF16, kind="Internal").ap()

    es = ExitStack()

    def alloc(name, cols, dtype=BF16):
        return es.enter_context(nc.sbuf_tensor(name, [128, cols], dtype))

    ident = alloc("identt", 128)
    maskr = alloc("maskr_t", H * 128)
    maskf = alloc("maskf_t", 128)
    maskl = alloc("maskl_t", 128)
    biasst = alloc("biasst", 128)
    obrep = alloc("obrep", D)
    obrow = es.enter_context(nc.sbuf_tensor("obrow", [1, D], BF16))
    onesrow = es.enter_context(nc.sbuf_tensor("onesrow", [1, 128], BF16))
    onesf = es.enter_context(nc.sbuf_tensor("onesf", [1, 128], F32))
    muscal = alloc("muscal", 16, F32)
    qscale = alloc("qscale", 1, F32)
    cexpb = alloc("cexpb", 1, F32)
    ceps = alloc("ceps", 1, F32)
    zcell = alloc("zcell", 8)
    tinyv = es.enter_context(nc.sbuf_tensor("tinyv", [1, 8], F32))
    lnsml = alloc("lnsml", 24, F32)
    kts = alloc("kts", 8 * 1024)
    vs = alloc("vs", 8 * 2048)
    A1 = alloc("A1", 24576)
    A2 = alloc("A2", 16384)
    A3 = alloc("A3", 25600)

    nc.const_aps.register(F32, 1e-5, ceps[:, 0:1])

    ps_f = es.enter_context(nc.psum_tensor("ps_f", [128, 3072], F32))
    ps_b = es.enter_context(nc.psum_tensor("ps_b", [128, 2048], BF16))

    def fb(bank, cols=512):
        return ps_f[:, bank * 512: bank * 512 + cols]

    kw_sb = A1[:, 0:8192]
    vw_sb = A1[:, 8192:16384]
    wstage = A1[:, 16384:24576]
    xbuf = [A2[:, 0:8192].bitcast(F32), A2[:, 8192:16384].bitcast(F32)]
    xntp = A3[:, 0:4096]
    xnst = A3[:, 4096:8192]
    xnbf = A3[:, 8192:9216]
    ktcst = A3[:, 9216:12288]
    v64st = [A3[:, 12288:13312], A3[:, 13312:14336]]
    wtevac = A3[:, 14336:15360]
    qw_sb = A2[:, 0:8192]
    ow_sb = A2[:, 8192:16384]
    bandk = A1[:, 0:3328]
    bandvi = [A1[:, 4096 + i * 2048: 4096 + (i + 1) * 2048] for i in range(4)]
    bandpm = [A1[:, 12288 + i * 2048: 12288 + (i + 1) * 2048] for i in range(4)]
    bandp = A1[:, 20480:22528]
    bandvp = A1[:, 22528:23552]
    xnt_b = A3[:, 0:4096]
    qte = A3[:, 4096:8192]
    qto = A3[:, 8192:12288]
    attnt = A3[:, 12288:16384]
    ptil = [A3[:, 16384 + i * 1024: 16384 + (i + 1) * 1024] for i in range(2)] + [A3[:, 24576:25600]]
    recb = A3[:, 18432:20480].bitcast(F32)
    lnb = A3[:, 20480:22528].bitcast(F32)
    outst = A3[:, 22528:24576].bitcast(F32)
    wrt = A1[:, 0:8192]
    wkt = A1[:, 8192:16384]
    wvt = A1[:, 16384:24576]
    xnt_c = A3[:, 0:4104]
    xr_c = A3[:, 4104:8200]
    xk_c = A3[:, 8200:12296]
    kact = A3[:, 12296:16392]
    dtmp = A3[:, 16392:16904]
    krel = A3[:, 16904:17416]
    rt_c = A3[:, 17416:18440].bitcast(F32)
    outc = A3[:, 18440:22536].bitcast(F32)

    p = Prog(nc, es)
    OP = p.op

    OP("sp", lambda e: e.dma_start(out=ident[:], in_=ident_h), writes=["ident"], dma="ident")
    OP("sp", lambda e: e.dma_start(out=maskr[:], in_=maskr_h), writes=["maskr"], dma="maskr")
    OP("sp", lambda e: e.dma_start(out=maskf[:], in_=maskf_h), writes=["maskf"], dma="maskf")
    OP("sp", lambda e: e.dma_start(out=maskl[:], in_=maskl_h), writes=["maskl"], dma="maskl")
    OP("sp", lambda e: e.dma_start(out=muscal[:, 0:8], in_=mu_r.rearrange("(i p) -> p i", p=128),
                                   allow_slow_non_contiguous=True),
       writes=["mu"], dma="mu")
    OP("sp", lambda e: e.dma_start(out=muscal[:, 8:16], in_=mu_k.rearrange("(i p) -> p i", p=128),
                                   allow_slow_non_contiguous=True),
       writes=["mu2"], dma="mu2")
    OP("sp", lambda e: e.dma_start(out=tinyv[0:1, 0:1], in_=us), writes=["usv"], dma="usv")
    OP("gp", lambda e: e.dma_start(out=kw_sb.rearrange("p (i m) -> p i m", i=8),
                                   in_=k_w.rearrange("(i p) m -> p i m", p=128)),
       writes=["kw"], dma="kw")
    OP("gp", lambda e: e.dma_start(out=vw_sb.rearrange("p (i m) -> p i m", i=8),
                                   in_=v_w.rearrange("(i p) m -> p i m", p=128)),
       writes=["vw"], dma="vw")
    OP("gp", lambda e: e.dma_start(out=obrow[:], in_=o_b), writes=["obrow"], dma="obrow")

    OP("dve", lambda e: e.memset(ceps[:, 0:1], 1e-5), writes=["ceps"])
    OP("dve", lambda e: e.memset(zcell[:], 0.0), writes=["zcell"])
    OP("dve", lambda e: e.memset(onesrow[:], 1.0), writes=["onesrow"])
    OP("dve", lambda e: e.memset(biasst[:], 1.0 / 128.0), writes=["biasst"])
    OP("dve", lambda e: e.memset(
        vs[:].rearrange("p (k h two s) -> p k h two s", k=8, h=H, two=2)[:, :, :, 1, :], 1.0),
        writes=["vs"])
    OP("act", lambda e: e.activation(tinyv[0:1, 1:2], tinyv[0:1, 0:1], AF.Exp, scale=-1.0),
       reads=["usv"], writes=["usv1"])
    OP("dve", lambda e: e.tensor_scalar(tinyv[0:1, 2:3], tinyv[0:1, 1:2], 1.0, None, ALU.add),
       reads=["usv1"], writes=["usv2"])
    OP("dve", lambda e: e.reciprocal(tinyv[0:1, 3:4], tinyv[0:1, 2:3]),
       reads=["usv2"], writes=["usv3"])
    OP("dve", lambda e: e.tensor_scalar(tinyv[0:1, 4:5], tinyv[0:1, 3:4], 0.125, None, ALU.mult),
       reads=["usv3"], writes=["usv4"])
    OP("dve", lambda e: e.memset(onesf[:], 1.0), writes=["onesf"])
    OP("pe", lambda e: e.matmul(fb(1, 1), onesf[:], tinyv[0:1, 4:5], start=True, stop=True),
       reads=["usv4", "onesf"], writes=["bank1"])
    OP("dve", lambda e: e.tensor_copy(qscale[:, 0:1], fb(1, 1)),
       reads=["bank1"], writes=["qscale"])
    OP("gp", lambda e: e.dma_start(out=xnT_scr[:, :, 0:1].rearrange("i p one -> p (i one)"),
                                   in_=zcell[:], allow_slow_non_contiguous=True),
       reads=["zcell"], dma="zscr")
    for half in range(2):
        OP("pe", lambda e, h=half: e.matmul(fb(0), onesrow[:],
                                            obrow[0:1, h * 512:(h + 1) * 512],
                                            start=True, stop=True),
           reads=["obrow", "onesrow"], writes=["bank0"])
        OP("dve", lambda e, h=half: e.tensor_copy(obrep[:, h * 512:(h + 1) * 512], fb(0)),
           reads=["bank0"], writes=["obrep"])

    for wi, hbm in enumerate((w_r, w_k, w_v)):
        OP("gp", lambda e, hbm=hbm: e.dma_start(
            out=wstage.rearrange("p (i m) -> p i m", i=8),
            in_=hbm.rearrange("(i p) m -> p i m", p=128)), writes=["wstage"], dma="wstage")
        for i in range(8):
            for m in range(8):
                OP("pe", lambda e, i=i, m=m: e.transpose(
                    ps_b[:, m * 128:(m + 1) * 128],
                    wstage[:, m * 1024 + i * 128: m * 1024 + (i + 1) * 128], ident[:]),
                    reads=["wstage", "ident"], writes=["bank6"])
            if wi == 2:
                OP("dve", lambda e: e.tensor_scalar(wtevac[:], ps_b[:, 0:1024], 0.5,
                                                    None, ALU.mult),
                   reads=["bank6"], writes=["wtevac"])
            else:
                OP("dve", lambda e: e.tensor_copy(wtevac[:], ps_b[:, 0:1024]),
                   reads=["bank6"], writes=["wtevac"])
            OP("gp", lambda e, wi=wi, i=i: e.dma_start(
                out=wt_scr[wi, :, i * 1024:(i + 1) * 1024], in_=wtevac[:]),
                reads=["wtevac"], dma=f"wts{wi}")

    for t in range(NT):
        xb = xbuf[t % 2]
        xslot = f"xb{t % 2}"
        xg = x.rearrange("(c g f) d -> c g f d", g=32, f=4)
        og = out.rearrange("(c g f) d -> c g f d", g=32, f=4)
        for c4 in range(4):
            c = 4 * t + c4
            OP("sp", lambda e, c=c, c4=c4, xg=xg, xb=xb: e.dma_start(
                out=xb[0:32, c4 * 1024:(c4 + 1) * 1024],
                in_=xg[c, :, 0, :]), writes=[xslot], dma=xslot)
            OP("sp", lambda e, c=c, c4=c4, xg=xg, xb=xb: e.dma_start(
                out=xb[32:128, c4 * 1024:(c4 + 1) * 1024],
                in_=xg[c, :, 1:4, :]),
                writes=[xslot], dma=xslot)
            OP("gp", lambda e, c=c, c4=c4, og=og, xb=xb: e.dma_start(
                out=og[c, :, 0, :],
                in_=xb[0:32, c4 * 1024:(c4 + 1) * 1024]),
                reads=[xslot], writes=[f"outr{t}"], dma="pre")
            OP("gp", lambda e, c=c, c4=c4, og=og, xb=xb: e.dma_start(
                out=og[c, :, 1:4, :],
                in_=xb[32:128, c4 * 1024:(c4 + 1) * 1024]),
                reads=[xslot], writes=[f"outr{t}"], dma="pre")
        for c4 in range(4):
            c = 4 * t + c4
            xc = xb[:, c4 * 1024:(c4 + 1) * 1024]
            OP("dve", lambda e, xc=xc: e.bn_stats(lnsml[:, 0:6], xc[:, 0:512]),
               reads=[xslot], writes=["lns"])
            OP("dve", lambda e, xc=xc: e.bn_stats(lnsml[:, 6:12], xc[:, 512:1024]),
               reads=[xslot], writes=["lns2"])
            OP("dve", lambda e: e.bn_aggr(lnsml[:, 12:14], lnsml[:, 0:12]),
               reads=["lns", "lns2"], writes=["mv"])
            OP("act", lambda e: e.activation(lnsml[:, 14:15], lnsml[:, 13:14], AF.Ln,
                                             bias=1e-5), reads=["mv", "ceps"], writes=["lnv"])
            OP("act", lambda e: e.activation(lnsml[:, 15:16], lnsml[:, 14:15], AF.Exp,
                                             scale=-0.5), reads=["lnv"], writes=["rstd"])
            OP("dve", lambda e, xc=xc: e.tensor_scalar(
                xnbf[:], xc, lnsml[:, 12:13], lnsml[:, 15:16],
                ALU.subtract, ALU.mult),
               reads=[xslot, "mv", "rstd"], writes=["xnbf"])
            for j in range(8):
                OP("pe", lambda e, j=j: e.transpose(
                    ps_b[:, 1024 + j * 128: 1024 + (j + 1) * 128],
                    xnbf[:, j * 128:(j + 1) * 128], ident[:]),
                    reads=["xnbf", "ident"], writes=["bank7"])
            OP("dve", lambda e, c4=c4: e.tensor_copy(
                xntp[:].rearrange("p (i w) -> p i w", i=8)[:, :, c4 * 128:(c4 + 1) * 128],
                ps_b[:, 1024:2048].rearrange("p (i w) -> p i w", i=8)),
                reads=["bank7"], writes=["xntp"])
            OP("dve", lambda e, c4=c4: e.tensor_copy(
                xnst[:].rearrange("p (i w) -> p i w", i=8)[:, :, c4 * 128:(c4 + 1) * 128]
                    .rearrange("p i (g o) -> p i g o", o=4)[:, :, :, 0:1],
                ps_b[:, 1024:2048].rearrange("p (i w) -> p i w", i=8)[:, :, 0:32]
                    .rearrange("p i (g o) -> p i g o", o=1)),
                reads=["bank7"], writes=["xnst"])
            OP("dve", lambda e, c4=c4: e.tensor_copy(
                xnst[:].rearrange("p (i w) -> p i w", i=8)[:, :, c4 * 128:(c4 + 1) * 128]
                    .rearrange("p i (g o) -> p i g o", o=4)[:, :, :, 1:4],
                ps_b[:, 1024:2048].rearrange("p (i w) -> p i w", i=8)[:, :, 32:128]
                    .rearrange("p i (g o) -> p i g o", o=3)),
                reads=["bank7"], writes=["xnst"])
        OP("gp", lambda e, t=t: e.dma_start(
            out=xnT_scr[:, :, t * 512 + 1: (t + 1) * 512 + 1].rearrange("i p w -> p i w"),
            in_=xnst[:].rearrange("p (i w) -> p i w", i=8)),
            reads=["xnst"], dma="xnts")
        for m in range(8):
            pb = f"bank{m % 2}"
            for i in range(8):
                OP("pe", lambda e, m=m, i=i: e.matmul(
                    fb(m % 2), kw_sb[:, i * 1024 + m * 128: i * 1024 + (m + 1) * 128],
                    xntp[:, i * 512:(i + 1) * 512], start=(i == 0), stop=(i == 7)),
                    reads=["kw", "xntp"], writes=[pb])
            OP("dve", lambda e, m=m, pb=pb, t=t: e.tensor_copy(
                kts[:, m * 1024 + t * 128: m * 1024 + (t + 1) * 128]
                    .rearrange("p (s w) -> p s w", s=4),
                fb(m % 2).rearrange("p (s w) -> p s w", s=4)[:, :, 0:32]),
                reads=[f"bank{m % 2}"], writes=["kts"])
            OP("dve", lambda e, m=m, pb=pb: e.tensor_copy(
                ktcst[:, m * 384:(m + 1) * 384].rearrange("p (s w) -> p s w", s=4),
                fb(m % 2).rearrange("p (s w) -> p s w", s=4)[:, :, 32:128]),
                reads=[f"bank{m % 2}"], writes=["ktcst"])
        OP("gp", lambda e, t=t: e.dma_start(
            out=ktc_scr[:, :, t * 384:(t + 1) * 384].rearrange("i p w -> p i w"),
            in_=ktcst[:].rearrange("p (i w) -> p i w", i=8)),
            reads=["ktcst"], dma="ktcs")
        for c4 in range(4):
            cg = 4 * t + c4
            pb = 2 + 2 * (c4 % 2)
            for half in range(2):
                for i in range(8):
                    OP("pe", lambda e, c4=c4, half=half, i=i, pb=pb: e.matmul(
                        ps_f[:, pb * 512 + half * 512: pb * 512 + (half + 1) * 512],
                        xntp[:, i * 512 + c4 * 128: i * 512 + (c4 + 1) * 128],
                        vw_sb[:, i * 1024 + half * 512: i * 1024 + (half + 1) * 512],
                        start=(i == 0), stop=(i == 7)),
                        reads=["vw", "xntp"], writes=[f"bank{pb + half}"])
            OP("dve", lambda e, cg=cg, pb=pb: e.tensor_copy(
                vs[32 * (cg % 4): 32 * (cg % 4) + 32,
                   (cg // 4) * 2048: (cg // 4) * 2048 + 2048]
                    .rearrange("p (h two s) -> p h two s", h=H, two=2)[:, :, 0:1, :],
                ps_f[0:32, pb * 512: pb * 512 + 1024]
                    .rearrange("p (h two s) -> p h two s", two=1, s=64)),
                reads=[f"bank{pb}", f"bank{pb+1}"], writes=["vs"])
            OP("dve", lambda e, v64=v64st[c4 % 2], pb=pb: e.tensor_copy(
                v64[32:64, :], ps_f[32:64, pb * 512: pb * 512 + 1024]),
                reads=[f"bank{pb}", f"bank{pb+1}"], writes=[f"v64st{c4 % 2}"])
            OP("dve", lambda e, v64=v64st[c4 % 2], pb=pb: e.tensor_copy(
                v64[64:128, :], ps_f[64:128, pb * 512: pb * 512 + 1024]),
                reads=[f"bank{pb}", f"bank{pb+1}"], writes=[f"v64st{c4 % 2}"])
            OP("gp", lambda e, cg=cg, v64=v64st[c4 % 2]: e.dma_start(
                out=v64_scr[cg * 96:(cg + 1) * 96, :], in_=v64[32:128, :]),
                reads=[f"v64st{c4 % 2}"], dma=f"v64w{c4 % 2}")

    p.barrier()

    run_b = "b" in stages
    run_c = "c" in stages
    blevel = 4
    for ch in stages:
        if ch.isdigit():
            blevel = int(ch)
    import os as _os
    bandlv = int(_os.environ.get("BANDLV", "4"))
    if run_b:

        OP("gp", lambda e: e.dma_start(out=qw_sb.rearrange("p (i m) -> p i m", i=8),
                                       in_=q_w.rearrange("(i p) m -> p i m", p=128)),
           writes=["qw"], dma="qw")
        OP("gp", lambda e: e.dma_start(out=ow_sb.rearrange("p (i m) -> p i m", i=8),
                                       in_=o_w.rearrange("(i p) m -> p i m", p=128)),
           writes=["ow"], dma="ow")
        OP("dve", lambda e: e.memset(qte[64:128, :], 0.0), writes=["qte"])
        OP("dve", lambda e: e.memset(qto[0:64, :], 0.0), writes=["qto"])
        OP("dve", lambda e: e.memset(bandk[:], 0.0), writes=["bandk"])
        OP("dve", lambda e: e.memset(bandvp[:], 0.0), writes=["bandvp"])
        for sg in range(4):
            OP("dve", lambda e, sg=sg: e.memset(bandvi[sg][:], 0.0), writes=[f"bandvi{sg}"])
            OP("dve", lambda e, sg=sg: e.memset(
                bandvi[sg].rearrange("p (h two s) -> p h two s", h=H, two=2)[:, :, 1, :], 1.0),
                writes=[f"bandvi{sg}"])

        for t in range(NT):
            OP("sp", lambda e, t=t: e.dma_start(
                out=xnt_b[:].rearrange("p (i w) -> p i w", i=8),
                in_=xnT_scr[:, :, t * 512 + 1:(t + 1) * 512 + 1].rearrange("i p w -> p i w")),
                writes=["xnt_b"], dma="xnt_b")
            lo = 384 * t - 6
            s0, s1 = max(lo, 0), min(lo + 416, CK)
            OP("sp", lambda e, lo=lo, s0=s0, s1=s1: e.dma_start(
                out=bandk[:].rearrange("p (i w) -> p i w", i=8)[:, :, s0 - lo: s1 - lo],
                in_=ktc_scr[:, :, s0:s1].rearrange("i p w -> p i w")),
                writes=["bandk"], dma="bandk")
            if t == NT - 1:
                pass
            for m in range(8):
                pb = f"bank{m % 2}"
                for i in range(8):
                    OP("pe", lambda e, m=m, i=i: e.matmul(
                        fb(m % 2), qw_sb[:, i * 1024 + m * 128: i * 1024 + (m + 1) * 128],
                        xnt_b[:, i * 512:(i + 1) * 512], start=(i == 0), stop=(i == 7)),
                        reads=["qw", "xnt_b"], writes=[pb])
                OP("dve", lambda e, m=m: e.tensor_copy(
                    qte[0:64, m * 512:(m + 1) * 512], fb(m % 2)[0:64, :]),
                   reads=[pb], writes=["qte"])
                OP("dve", lambda e, m=m: e.tensor_copy(
                    qto[64:128, m * 512:(m + 1) * 512], fb(m % 2)[64:128, :]),
                   reads=[pb], writes=["qto"])
            for sg in range(4 if blevel >= 2 else 0):
                sglob = 4 * t + sg
                vlo = 96 * sglob - 6
                v0, v1 = max(vlo, 0), min(vlo + 128, CK)
                OP("sp", lambda e, vlo=vlo, v0=v0, v1=v1: e.dma_start(
                    out=bandvp[v0 - vlo: v0 - vlo + (v1 - v0), :], in_=v64_scr[v0:v1, :]),
                    writes=["bandvp"], dma="bandvp")
                OP("dve", lambda e, sg=sg: e.tensor_copy(
                    bandvi[sg].rearrange("p (h two s) -> p h two s", h=H, two=2)[:, :, 0:1, :],
                    bandvp[:].rearrange("p (h two s) -> p h two s", two=1, s=64)),
                    reads=["bandvp"], writes=[f"bandvi{sg}"])
                for halfh in range(2):
                    pb2 = 2 * halfh
                    for hh in range(8 if bandlv >= 2 else 0):
                        h = halfh * 8 + hh
                        qsrc = qte if h % 2 == 0 else qto
                        qname = "qte" if h % 2 == 0 else "qto"
                        OP("pe", lambda e, h=h, hh=hh, sg=sg, pb2=pb2, qsrc=qsrc: e.matmul(
                            ps_f[:, pb2 * 512 + hh * 128: pb2 * 512 + (hh + 1) * 128],
                            bandk[:, (h // 2) * 416 + 96 * sg:
                                  (h // 2) * 416 + 96 * sg + 128],
                            qsrc[:, (h // 2) * 512 + sg * 128:
                                 (h // 2) * 512 + (sg + 1) * 128],
                            start=True, stop=True),
                            reads=["bandk", qname], writes=[f"bank{pb2 + hh // 4}"])
                    if bandlv >= 3:
                        OP("act", lambda e, halfh=halfh, pb2=pb2: e.activation(
                            bandp[:, halfh * 1024:(halfh + 1) * 1024],
                            ps_f[:, pb2 * 512: pb2 * 512 + 1024], AF.Exp,
                            scale=qscale[:, 0:1]),
                            reads=[f"bank{pb2}", f"bank{pb2 + 1}", "qscale"],
                            writes=["bandp"])
                if bandlv < 4:
                    pass
                elif sglob == 0 or sglob == NT * 4 - 1:
                    mk = maskf if sglob == 0 else maskl
                    mname = "maskf" if sglob == 0 else "maskl"
                    for h in range(H):
                        OP("dve", lambda e, sg=sg, h=h, mk=mk: e.tensor_mul(
                            bandpm[sg][:, h * 128:(h + 1) * 128],
                            bandp[:, h * 128:(h + 1) * 128], mk[:]),
                            reads=["bandp", mname], writes=[f"bandpm{sg}"])
                else:
                    OP("dve", lambda e, sg=sg: e.tensor_mul(bandpm[sg][:], bandp[:], maskr[:]),
                       reads=["bandp", "maskr"], writes=[f"bandpm{sg}"])
            for pr in range(8 if blevel >= 3 else 0):
                he, ho = 2 * pr, 2 * pr + 1
                def emit_scores(g):
                    pb2 = 2 * (g % 2)
                    for (par, tp_c, kk) in ((0, 0, 2 * g), (0, 64, 2 * g + 1),
                                            (1, 0, 2 * g), (1, 64, 2 * g + 1)):
                        qsrc = qte if par == 0 else qto
                        qname = "qte" if par == 0 else "qto"
                        OP("pe", lambda e, par=par, tp_c=tp_c, kk=kk, pr=pr, pb2=pb2,
                            qsrc=qsrc: e.matmul(
                            ps_f[tp_c:tp_c + 64, pb2 * 512 + par * 512:
                                 pb2 * 512 + par * 512 + 512],
                            kts[:, pr * 1024 + kk * 64: pr * 1024 + (kk + 1) * 64],
                            qsrc[:, pr * 512:(pr + 1) * 512],
                            start=True, stop=True, tile_position=(0, tp_c)),
                            reads=["kts", qname], writes=[f"bank{pb2 + par}"])
                    OP("act", lambda e, g=g, pb2=pb2: e.activation(
                        ptil[g % 3][:], ps_f[:, pb2 * 512: pb2 * 512 + 1024], AF.Exp,
                        scale=qscale[:, 0:1]),
                        reads=[f"bank{pb2}", f"bank{pb2 + 1}", "qscale"],
                        writes=[f"ptil{g % 3}"])

                def emit_av(g):
                    for sub, col in ((0, 0), (1, 512)):
                        OP("pe", lambda e, g=g, pr=pr, sub=sub, col=col: e.matmul(
                            fb(4 + sub), vs[:, g * 2048 + (2 * pr + sub) * 128:
                                            g * 2048 + (2 * pr + sub) * 128 + 128],
                            ptil[g % 3][:, col:col + 512],
                            start=(g == 0), stop=False),
                            reads=[f"ptil{g % 3}", "vs"], writes=[f"bank{4 + sub}"])

                emit_scores(0)
                emit_scores(1)
                for g in range(2, 8):
                    emit_scores(g)
                    emit_av(g - 2)
                emit_av(6)
                emit_av(7)
                for sg in range(4 if blevel >= 2 else 0):
                    for sub in range(2):
                        h = 2 * pr + sub
                        OP("pe", lambda e, sg=sg, h=h, sub=sub, last=(sg == 3): e.matmul(
                            ps_f[:, (4 + sub) * 512 + sg * 128: (4 + sub) * 512 + (sg + 1) * 128],
                            bandvi[sg][:, h * 128:(h + 1) * 128],
                            bandpm[sg][:, h * 128:(h + 1) * 128],
                            start=False, stop=last),
                            reads=[f"bandvi{sg}", f"bandpm{sg}"], writes=[f"bank{4 + sub}"])
                OP("act", lambda e: e.activation(lnb[0:64, :], ps_f[64:128, 2048:3072], AF.Ln),
                   reads=["bank4", "bank5"], writes=["lnb"])
                OP("act", lambda e: e.activation(recb[0:64, :], lnb[0:64, :], AF.Exp, scale=-1.0),
                   reads=["lnb"], writes=["recb"])
                OP("dve", lambda e, pr=pr: e.tensor_mul(
                    attnt[0:64, pr * 512:(pr + 1) * 512], ps_f[0:64, 2048:2560],
                    recb[0:64, 0:512]),
                    reads=["bank4", "recb"], writes=["attnt"])
                OP("dve", lambda e, pr=pr: e.tensor_mul(
                    attnt[64:128, pr * 512:(pr + 1) * 512], ps_f[0:64, 2560:3072],
                    recb[0:64, 512:1024]),
                    reads=["bank5", "recb"], writes=["attnt"])
            for sub in range(4 if blevel >= 4 and blevel >= 3 else 0):
                for half in range(2):
                    pb = half
                    pname = f"bank{half}"
                    OP("pe", lambda e, half=half, pb=pb: e.matmul(
                        fb(pb), biasst[:], obrep[:, half * 512:(half + 1) * 512],
                        start=True, stop=False),
                        reads=["biasst", "obrep"], writes=[pname])
                    for i in range(8):
                        OP("pe", lambda e, sub=sub, half=half, i=i, pb=pb: e.matmul(
                            fb(pb), attnt[:, i * 512 + sub * 128: i * 512 + sub * 128 + 128],
                            ow_sb[:, i * 1024 + half * 512: i * 1024 + (half + 1) * 512],
                            start=False, stop=(i == 7)),
                            reads=["attnt", "ow"], writes=[pname])
                    OP("dve", lambda e, half=half, pb=pb: e.tensor_copy(
                        outst[:, half * 512:(half + 1) * 512], fb(pb)),
                        reads=[pname], writes=["outst"])
                OP("gp", lambda e, t=t, sub=sub: e.dma_start(
                    out=out[t * 512 + sub * 128: t * 512 + (sub + 1) * 128, :],
                    in_=outst[:, 0:1024], accum_op=ALU.add),
                    reads=["outst"], writes=[f"outr{t}"], dma="oacc")

    p.barrier()

    if run_c:
        for wi, dst in enumerate((wrt, wkt, wvt)):
            OP("sp", lambda e, wi=wi, dst=dst: e.dma_start(
                out=dst.rearrange("p (i m) -> p i m", i=8),
                in_=wt_scr[wi].rearrange("p (i m) -> p i m", i=8)),
                writes=[f"wt{wi}"], dma=f"wt{wi}")
        for t in range(NT):
            OP("sp", lambda e, t=t: e.dma_start(
                out=xnt_c[:].rearrange("p (i w) -> p i w", i=8),
                in_=xnT_scr[:, :, t * 512:(t + 1) * 512 + 1].rearrange("i p w -> p i w")),
                writes=["xnt_c"], dma="xnt_c")
            for i in range(8):
                cur = xnt_c[:, i * 513 + 1: i * 513 + 513]
                prev = xnt_c[:, i * 513: i * 513 + 512]
                OP("dve", lambda e, cur=cur, prev=prev: e.tensor_sub(dtmp[:], cur, prev),
                   reads=["xnt_c"], writes=["dtmp"])
                OP("dve", lambda e, i=i, prev=prev: e.scalar_tensor_tensor(
                    xr_c[:, i * 512:(i + 1) * 512], dtmp[:], muscal[:, i:i + 1], prev,
                    ALU.mult, ALU.add),
                    reads=["dtmp", "mu", "xnt_c"], writes=["xr"])
                OP("dve", lambda e, i=i, prev=prev: e.scalar_tensor_tensor(
                    xk_c[:, i * 512:(i + 1) * 512], dtmp[:], muscal[:, 8 + i:9 + i], prev,
                    ALU.mult, ALU.add),
                    reads=["dtmp", "mu2", "xnt_c"], writes=["xk"])
            for m in range(8):
                pname = f"bank{m % 2}"
                for i in range(8):
                    OP("pe", lambda e, m=m, i=i: e.matmul(
                        fb(m % 2), wkt[:, i * 1024 + m * 128: i * 1024 + (m + 1) * 128],
                        xk_c[:, i * 512:(i + 1) * 512], start=(i == 0), stop=(i == 7)),
                        reads=["wt1", "xk"], writes=[pname])
                OP("dve", lambda e, m=m: e.tensor_scalar(krel[:], fb(m % 2), 0.0, None, ALU.max),
                   reads=[pname], writes=["krel"])
                OP("dve", lambda e, m=m: e.tensor_mul(kact[:, m * 512:(m + 1) * 512],
                                                      krel[:], krel[:]),
                   reads=["krel"], writes=["kact"])
            for sub in range(4):
                for half in range(2):
                    pr_ = f"bank{2 + half}"
                    pv_ = f"bank{4 + half}"
                    for i in range(8):
                        OP("pe", lambda e, sub=sub, half=half, i=i: e.matmul(
                            fb(2 + half), xr_c[:, i * 512 + sub * 128: i * 512 + sub * 128 + 128],
                            wrt[:, i * 1024 + half * 512: i * 1024 + (half + 1) * 512],
                            start=(i == 0), stop=(i == 7)),
                            reads=["wt0", "xr"], writes=[pr_])
                    OP("act", lambda e, half=half: e.activation(rt_c[:], fb(2 + half), AF.Tanh,
                                                                scale=0.5),
                       reads=[pr_], writes=["rt"])
                    for i in range(8):
                        OP("pe", lambda e, sub=sub, half=half, i=i: e.matmul(
                            fb(4 + half), kact[:, i * 512 + sub * 128: i * 512 + sub * 128 + 128],
                            wvt[:, i * 1024 + half * 512: i * 1024 + (half + 1) * 512],
                            start=(i == 0), stop=(i == 7)),
                            reads=["wt2", "kact"], writes=[pv_])
                    OP("dve", lambda e, half=half: e.scalar_tensor_tensor(
                        outc[:, half * 512:(half + 1) * 512], rt_c[:], 1.0, fb(4 + half),
                        ALU.add, ALU.mult),
                        reads=["rt", pv_], writes=["outc"])
                OP("gp", lambda e, t=t, sub=sub: e.dma_start(
                    out=out[t * 512 + sub * 128: t * 512 + (sub + 1) * 128, :],
                    in_=outc[:, 0:1024], accum_op=ALU.add),
                    reads=["outc"], writes=[f"outr{t}"], dma="oaccC")

    slots = ["pre"] + (["oacc"] if run_b and blevel >= 4 else []) + (["oaccC"] if run_c else [])
    p.final_wait("gp", slots)

    p.prepare()
    with nc.Block() as block:
        @block.sync
        def _(eng):
            p.emit_engine("sp", eng)

        @block.gpsimd
        def _(eng):
            p.emit_engine("gp", eng)

        @block.vector
        def _(eng):
            p.emit_engine("dve", eng)

        @block.scalar
        def _(eng):
            p.emit_engine("act", eng)

        @block.tensor
        def _(eng):
            p.emit_engine("pe", eng)

    es.close()
    return nc


def make_inputs(inp):
    f32 = np.float32
    maskr, maskf, maskl = _band_masks()
    d = {
        "q_w": np.ascontiguousarray(inp["L0_attn_q_w"], f32),
        "k_w": np.ascontiguousarray(inp["L0_attn_k_w"], f32),
        "v_w": np.ascontiguousarray(inp["L0_attn_v_w"], f32),
        "o_w": np.ascontiguousarray(inp["L0_attn_o_w"], f32),
        "o_b": np.ascontiguousarray(inp["L0_attn_o_b"], f32).reshape(1, D),
        "us": np.asarray(inp["L0_attn_unity_scale"], f32).reshape(1, 1),
        "w_r": np.ascontiguousarray(inp["L0_ffn_W_r"], f32),
        "w_k": np.ascontiguousarray(inp["L0_ffn_W_k"], f32),
        "w_v": np.ascontiguousarray(inp["L0_ffn_W_v"], f32),
        "mu_r": np.ascontiguousarray(inp["L0_ffn_mu_r"], f32),
        "mu_k": np.ascontiguousarray(inp["L0_ffn_mu_k"], f32),
        "ident": np.eye(128, dtype=ml_dtypes.bfloat16),
        "maskr": maskr, "maskf": maskf, "maskl": maskl,
    }
    return d


_NC_CACHE = [None]


def kernel(**inputs):
    import os
    if _NC_CACHE[0] is None:
        _NC_CACHE[0] = build_nc(os.environ.get("KSTAGES", "abc"))
    nc = _NC_CACHE[0]
    base = make_inputs(inputs)
    xfull = np.ascontiguousarray(inputs["L0_attn_x"], np.float32)
    in_maps = []
    for c in range(NC):
        m = dict(base)
        m["x"] = xfull[c]
        in_maps.append(m)
    res = run_bass_kernel_spmd(nc, in_maps, core_ids=list(range(NC)))
    return np.stack([r["out"] for r in res.results], axis=0)


if __name__ == "__main__":
    rng = np.random.default_rng(0)
    fake = {
        "L0_attn_x": rng.standard_normal((NC, S, D), dtype=np.float32),
        "L0_attn_q_w": rng.standard_normal((D, D), dtype=np.float32) / 32,
        "L0_attn_k_w": rng.standard_normal((D, D), dtype=np.float32) / 32,
        "L0_attn_v_w": rng.standard_normal((D, D), dtype=np.float32) / 32,
        "L0_attn_o_w": rng.standard_normal((D, D), dtype=np.float32) / 32,
        "L0_attn_o_b": rng.standard_normal(D).astype(np.float32) * 0.01,
        "L0_attn_unity_scale": np.float32(1.618),
        "L0_ffn_W_r": (rng.random((D, D), dtype=np.float32) - 0.5) / 16,
        "L0_ffn_W_k": (rng.random((D, D), dtype=np.float32) - 0.5) / 16,
        "L0_ffn_W_v": (rng.random((D, D), dtype=np.float32) - 0.5) / 16,
        "L0_ffn_mu_r": rng.random(D, dtype=np.float32),
        "L0_ffn_mu_k": rng.random(D, dtype=np.float32),
    }
    out = kernel(**fake)
    print("ran", out.shape, out.dtype)


# version 04
# speedup vs baseline: 1.5026x; 1.3043x over previous
import numpy as np
import ml_dtypes
import concourse.bass as bass
import concourse.mybir as mybir
from concourse.bass_utils import run_bass_kernel_spmd
from contextlib import ExitStack

F32 = mybir.dt.float32
BF16 = mybir.dt.bfloat16
AF = mybir.ActivationFunctionType
ALU = mybir.AluOpType

S, D, H, HD = 4096, 1024, 16, 64
TW = 512
NT = S // TW
NC = 8
CK = 3 * S // 4
EXPB = -8.0


def _band_masks():
    kr = np.arange(108)
    v = kr - 6
    jrel = 4 * (v // 3) + (v % 3) + 1
    qr = np.arange(128)
    m = (np.abs(qr[None, :] - jrel[:, None]) <= 8).astype(np.float32)
    first = m * (kr >= 6)[:, None]
    last = m * (kr < 102)[:, None]
    rep = np.zeros((128, H * 128), np.float32)
    rep[:108] = np.tile(m, (1, H))
    f = np.zeros((128, 128), np.float32)
    f[:108] = first
    l = np.zeros((128, 128), np.float32)
    l[:108] = last
    return (rep.astype(ml_dtypes.bfloat16), f.astype(ml_dtypes.bfloat16),
            l.astype(ml_dtypes.bfloat16))


class Prog:

    ENGINES = ("pe", "act", "dve", "sp", "gp")

    def __init__(self, nc, es):
        self.nc = nc
        self.es = es
        self.ops = {e: [] for e in self.ENGINES}
        self.sem = {e: es.enter_context(nc.semaphore(f"sem_{e}"))
                    for e in ("pe", "act", "dve")}
        self.dma_sems = {}
        self.bufs = {}
        self.milestones = {e: set() for e in ("pe", "act", "dve")}
        self.final_slots = []

    def _buf(self, name):
        return self.bufs.setdefault(name, {"w": None, "r": {}})

    def _addwait(self, eng, key, val, waits):
        if key == eng and key in self.sem:
            pass
        waits.append((key, val))
        if key in self.milestones:
            self.milestones[key].add(val)

    def op(self, eng, fn, reads=(), writes=(), dma=None):
        waits = []

        def need(k):
            return not (dma is None and eng == "pe" and k == "pe")

        for b in reads:
            st = self._buf(b)
            if st["w"] is not None and need(st["w"][0]):
                self._addwait(eng, *st["w"], waits)
        for b in writes:
            st = self._buf(b)
            if st["w"] is not None and need(st["w"][0]):
                self._addwait(eng, *st["w"], waits)
            for k, v in st["r"].items():
                if need(k):
                    self._addwait(eng, k, v, waits)

        if dma is not None:
            if dma not in self.dma_sems:
                self.dma_sems[dma] = [self.es.enter_context(
                    self.nc.semaphore(f"ds_{dma}")), 0]
            self.dma_sems[dma][1] += 16
            key, val = dma, self.dma_sems[dma][1]
        else:
            key, val = eng, len(self.ops[eng])

        for b in reads:
            self._buf(b)["r"][key] = val
        for b in writes:
            st = self._buf(b)
            st["w"] = (key, val)
            st["r"] = {}

        self.ops[eng].append((fn, waits, dma))

    def barrier(self):
        targets = [(e, len(self.ops[e]) - 1) for e in ("pe", "act", "dve")
                   if self.ops[e]]
        targets += [(k, v[1]) for k, v in self.dma_sems.items()]
        for eng in self.ENGINES:
            waits = []
            for key, val in targets:
                if key != eng:
                    self._addwait(eng, key, val, waits)
            self.ops[eng].append((None, waits, None))

    def final_wait(self, eng, slots):
        waits = [(sl, self.dma_sems[sl][1]) for sl in slots]
        self.ops[eng].append((None, waits, None))

    def prepare(self):
        self.rank = {e: {idx: i + 1 for i, idx in enumerate(sorted(ms))}
                     for e, ms in self.milestones.items()}
        self.water = {e: {} for e in self.ENGINES}

    def emit_engine(self, eng, engobj):
        rank = self.rank
        water = self.water
        for opi, (fn, waits, dma) in enumerate(self.ops[eng]):
            for key, val in waits:
                if key in rank:
                    v = rank[key].get(val)
                    assert v is not None, (eng, key, val)
                    semh = self.sem[key]
                else:
                    v = val
                    semh = self.dma_sems[key][0]
                if water[eng].get(key, 0) >= v:
                    continue
                water[eng][key] = v
                engobj.wait_ge(semh, v)
            if fn is None:
                continue
            inst = fn(engobj)
            if dma is not None:
                inst.then_inc(self.dma_sems[dma][0], 16)
            elif eng in rank and opi in rank[eng]:
                inst.then_inc(self.sem[eng], 1)


def build_nc(stages="abc"):
    nc = bass.Bass("TRN2", target_bir_lowering=False)

    x = nc.dram_tensor("x", [S, D], F32, kind="ExternalInput").ap()
    q_w = nc.dram_tensor("q_w", [D, D], F32, kind="ExternalInput").ap()
    k_w = nc.dram_tensor("k_w", [D, D], F32, kind="ExternalInput").ap()
    v_w = nc.dram_tensor("v_w", [D, D], F32, kind="ExternalInput").ap()
    o_w = nc.dram_tensor("o_w", [D, D], F32, kind="ExternalInput").ap()
    o_b = nc.dram_tensor("o_b", [1, D], F32, kind="ExternalInput").ap()
    us = nc.dram_tensor("us", [1, 1], F32, kind="ExternalInput").ap()
    w_r = nc.dram_tensor("w_r", [D, D], F32, kind="ExternalInput").ap()
    w_k = nc.dram_tensor("w_k", [D, D], F32, kind="ExternalInput").ap()
    w_v = nc.dram_tensor("w_v", [D, D], F32, kind="ExternalInput").ap()
    mu_r = nc.dram_tensor("mu_r", [D], F32, kind="ExternalInput").ap()
    mu_k = nc.dram_tensor("mu_k", [D], F32, kind="ExternalInput").ap()
    ident_h = nc.dram_tensor("ident", [128, 128], BF16, kind="ExternalInput").ap()
    maskr_h = nc.dram_tensor("maskr", [128, H * 128], BF16, kind="ExternalInput").ap()
    maskf_h = nc.dram_tensor("maskf", [128, 128], BF16, kind="ExternalInput").ap()
    maskl_h = nc.dram_tensor("maskl", [128, 128], BF16, kind="ExternalInput").ap()
    out = nc.dram_tensor("out", [S, D], F32, kind="ExternalOutput").ap()

    xnT_scr = nc.dram_tensor("xnT_scr", [8, 128, S + 1], BF16, kind="Internal").ap()
    ktc_scr = nc.dram_tensor("ktc_scr", [8, 128, CK], BF16, kind="Internal").ap()
    v64_scr = nc.dram_tensor("v64_scr", [CK, D], BF16, kind="Internal").ap()
    wt_scr = nc.dram_tensor("wt_scr", [3, 128, 8 * D], BF16, kind="Internal").ap()

    es = ExitStack()

    def alloc(name, cols, dtype=BF16):
        return es.enter_context(nc.sbuf_tensor(name, [128, cols], dtype))

    ident = alloc("identt", 128)
    maskr = alloc("maskr_t", H * 128)
    maskf = alloc("maskf_t", 128)
    maskl = alloc("maskl_t", 128)
    biasst = alloc("biasst", 128)
    obrep = alloc("obrep", D)
    obrow = es.enter_context(nc.sbuf_tensor("obrow", [1, D], BF16))
    onesrow = es.enter_context(nc.sbuf_tensor("onesrow", [1, 128], BF16))
    onesf = es.enter_context(nc.sbuf_tensor("onesf", [1, 128], F32))
    muscal = alloc("muscal", 16, F32)
    qscale = alloc("qscale", 1, F32)
    cexpb = alloc("cexpb", 1, F32)
    ceps = alloc("ceps", 1, F32)
    zcell = alloc("zcell", 8)
    tinyv = es.enter_context(nc.sbuf_tensor("tinyv", [1, 8], F32))
    lnsml = alloc("lnsml", 48, F32)
    kts = alloc("kts", 8 * 1024)
    vs = alloc("vs", 8 * 2048)
    A1 = alloc("A1", 24576)
    A2 = alloc("A2", 16384)
    A3 = alloc("A3", 25600)

    nc.const_aps.register(F32, 1e-5, ceps[:, 0:1])

    ps_f = es.enter_context(nc.psum_tensor("ps_f", [128, 3072], F32))
    ps_b = es.enter_context(nc.psum_tensor("ps_b", [128, 2048], BF16))

    def fb(bank, cols=512):
        return ps_f[:, bank * 512: bank * 512 + cols]

    kw_sb = A1[:, 0:8192]
    vw_sb = A1[:, 8192:16384]
    wstage = A1[:, 16384:24576]
    xbuf = [A2[:, 0:8192].bitcast(F32), A2[:, 8192:16384].bitcast(F32)]
    xntp = A3[:, 0:4096]
    xnst = A3[:, 4096:8192]
    xnbf2 = [A3[:, 8192:9216], A3[:, 15360:16384]]
    ktcst = A3[:, 9216:12288]
    v64st = [A3[:, 12288:13312], A3[:, 13312:14336]]
    wtevac = A3[:, 14336:15360]
    qw_sb = A2[:, 0:8192]
    ow_sb = A2[:, 8192:16384]
    bandk = A1[:, 0:3328]
    bandvi = [A1[:, 4096 + i * 2048: 4096 + (i + 1) * 2048] for i in range(4)]
    bandpm = [A1[:, 12288 + i * 2048: 12288 + (i + 1) * 2048] for i in range(4)]
    bandp = A1[:, 20480:22528]
    bandvp = A1[:, 22528:23552]
    xnt_b = A3[:, 0:4096]
    qte = A3[:, 4096:8192]
    qto = A3[:, 8192:12288]
    attnt = A3[:, 12288:16384]
    ptil = [A3[:, 16384 + i * 1024: 16384 + (i + 1) * 1024] for i in range(2)] + [A3[:, 24576:25600]]
    recb = A3[:, 18432:20480].bitcast(F32)
    lnb = A3[:, 20480:22528].bitcast(F32)
    outst = A3[:, 22528:24576].bitcast(F32)
    wrt = A1[:, 0:8192]
    wkt = A1[:, 8192:16384]
    wvt = A1[:, 16384:24576]
    xnt_c = A3[:, 0:4104]
    xr_c = A3[:, 4104:8200]
    xk_c = A3[:, 8200:12296]
    kact = A3[:, 12296:16392]
    dtmp = A3[:, 16392:16904]
    krel = A3[:, 16904:17416]
    rt_c = A3[:, 17416:18440].bitcast(F32)
    outc = A3[:, 18440:22536].bitcast(F32)

    p = Prog(nc, es)
    OP = p.op

    OP("sp", lambda e: e.dma_start(out=ident[:], in_=ident_h), writes=["ident"], dma="ident")
    OP("sp", lambda e: e.dma_start(out=maskr[:], in_=maskr_h), writes=["maskr"], dma="maskr")
    OP("sp", lambda e: e.dma_start(out=maskf[:], in_=maskf_h), writes=["maskf"], dma="maskf")
    OP("sp", lambda e: e.dma_start(out=maskl[:], in_=maskl_h), writes=["maskl"], dma="maskl")
    OP("sp", lambda e: e.dma_start(out=muscal[:, 0:8], in_=mu_r.rearrange("(i p) -> p i", p=128),
                                   allow_slow_non_contiguous=True),
       writes=["mu"], dma="mu")
    OP("sp", lambda e: e.dma_start(out=muscal[:, 8:16], in_=mu_k.rearrange("(i p) -> p i", p=128),
                                   allow_slow_non_contiguous=True),
       writes=["mu2"], dma="mu2")
    OP("sp", lambda e: e.dma_start(out=tinyv[0:1, 0:1], in_=us), writes=["usv"], dma="usv")
    OP("gp", lambda e: e.dma_start(out=kw_sb.rearrange("p (i m) -> p i m", i=8),
                                   in_=k_w.rearrange("(i p) m -> p i m", p=128)),
       writes=["kw"], dma="kw")
    OP("gp", lambda e: e.dma_start(out=vw_sb.rearrange("p (i m) -> p i m", i=8),
                                   in_=v_w.rearrange("(i p) m -> p i m", p=128)),
       writes=["vw"], dma="vw")
    OP("gp", lambda e: e.dma_start(out=obrow[:], in_=o_b), writes=["obrow"], dma="obrow")

    OP("dve", lambda e: e.memset(ceps[:, 0:1], 1e-5), writes=["ceps"])
    OP("dve", lambda e: e.memset(zcell[:], 0.0), writes=["zcell"])
    OP("dve", lambda e: e.memset(onesrow[:], 1.0), writes=["onesrow"])
    OP("dve", lambda e: e.memset(biasst[:], 1.0 / 128.0), writes=["biasst"])
    OP("dve", lambda e: e.memset(
        vs[:].rearrange("p (k h two s) -> p k h two s", k=8, h=H, two=2)[:, :, :, 1, :], 1.0),
        writes=["vs"])
    OP("act", lambda e: e.activation(tinyv[0:1, 1:2], tinyv[0:1, 0:1], AF.Exp, scale=-1.0),
       reads=["usv"], writes=["usv1"])
    OP("dve", lambda e: e.tensor_scalar(tinyv[0:1, 2:3], tinyv[0:1, 1:2], 1.0, None, ALU.add),
       reads=["usv1"], writes=["usv2"])
    OP("dve", lambda e: e.reciprocal(tinyv[0:1, 3:4], tinyv[0:1, 2:3]),
       reads=["usv2"], writes=["usv3"])
    OP("dve", lambda e: e.tensor_scalar(tinyv[0:1, 4:5], tinyv[0:1, 3:4], 0.125, None, ALU.mult),
       reads=["usv3"], writes=["usv4"])
    OP("dve", lambda e: e.memset(onesf[:], 1.0), writes=["onesf"])
    OP("pe", lambda e: e.matmul(fb(1, 1), onesf[:], tinyv[0:1, 4:5], start=True, stop=True),
       reads=["usv4", "onesf"], writes=["bank1"])
    OP("dve", lambda e: e.tensor_copy(qscale[:, 0:1], fb(1, 1)),
       reads=["bank1"], writes=["qscale"])
    OP("gp", lambda e: e.dma_start(out=xnT_scr[:, :, 0:1].rearrange("i p one -> p (i one)"),
                                   in_=zcell[:], allow_slow_non_contiguous=True),
       reads=["zcell"], dma="zscr")
    for half in range(2):
        OP("pe", lambda e, h=half: e.matmul(fb(0), onesrow[:],
                                            obrow[0:1, h * 512:(h + 1) * 512],
                                            start=True, stop=True),
           reads=["obrow", "onesrow"], writes=["bank0"])
        OP("dve", lambda e, h=half: e.tensor_copy(obrep[:, h * 512:(h + 1) * 512], fb(0)),
           reads=["bank0"], writes=["obrep"])

    for wi, hbm in enumerate((w_r, w_k, w_v)):
        OP("gp", lambda e, hbm=hbm: e.dma_start(
            out=wstage.rearrange("p (i m) -> p i m", i=8),
            in_=hbm.rearrange("(i p) m -> p i m", p=128)), writes=["wstage"], dma="wstage")
        for i in range(8):
            for m in range(8):
                OP("pe", lambda e, i=i, m=m: e.transpose(
                    ps_b[:, m * 128:(m + 1) * 128],
                    wstage[:, m * 1024 + i * 128: m * 1024 + (i + 1) * 128], ident[:]),
                    reads=["wstage", "ident"], writes=["bank6"])
            if wi == 2:
                OP("dve", lambda e: e.tensor_scalar(wtevac[:], ps_b[:, 0:1024], 0.5,
                                                    None, ALU.mult),
                   reads=["bank6"], writes=["wtevac"])
            else:
                OP("dve", lambda e: e.tensor_copy(wtevac[:], ps_b[:, 0:1024]),
                   reads=["bank6"], writes=["wtevac"])
            OP("gp", lambda e, wi=wi, i=i: e.dma_start(
                out=wt_scr[wi, :, i * 1024:(i + 1) * 1024], in_=wtevac[:]),
                reads=["wtevac"], dma=f"wts{wi}")

    for t in range(NT):
        xb = xbuf[t % 2]
        xg = x.rearrange("(c g f) d -> c g f d", g=32, f=4)
        og = out.rearrange("(c g f) d -> c g f d", g=32, f=4)
        for c4 in range(4):
            c = 4 * t + c4
            xslot = f"xb{t % 2}_{c4}"
            OP("sp", lambda e, c=c, c4=c4, xg=xg, xb=xb: e.dma_start(
                out=xb[0:32, c4 * 1024:(c4 + 1) * 1024],
                in_=xg[c, :, 0, :]), writes=[xslot], dma=xslot)
            OP("sp", lambda e, c=c, c4=c4, xg=xg, xb=xb: e.dma_start(
                out=xb[32:128, c4 * 1024:(c4 + 1) * 1024],
                in_=xg[c, :, 1:4, :]),
                writes=[xslot], dma=xslot)
            OP("gp", lambda e, c=c, c4=c4, og=og, xb=xb: e.dma_start(
                out=og[c, :, 0, :],
                in_=xb[0:32, c4 * 1024:(c4 + 1) * 1024]),
                reads=[xslot], writes=[f"outr{t}"], dma=f"pre{t % 2}_{c4}")
            OP("gp", lambda e, c=c, c4=c4, og=og, xb=xb: e.dma_start(
                out=og[c, :, 1:4, :],
                in_=xb[32:128, c4 * 1024:(c4 + 1) * 1024]),
                reads=[xslot], writes=[f"outr{t}"], dma=f"pre{t % 2}_{c4}")
        del xslot
        for c4 in range(4):
            c = 4 * t + c4
            xslot = f"xb{t % 2}_{c4}"
            xc = xb[:, c4 * 1024:(c4 + 1) * 1024]
            par = c4 % 2
            lo = 24 * par
            xnbf = xnbf2[par]
            OP("dve", lambda e, xc=xc, lo=lo: e.bn_stats(lnsml[:, lo:lo + 6], xc[:, 0:512]),
               reads=[xslot], writes=[f"lns{par}"])
            OP("dve", lambda e, xc=xc, lo=lo: e.bn_stats(lnsml[:, lo + 6:lo + 12], xc[:, 512:1024]),
               reads=[xslot], writes=[f"lns{par}b"])
            OP("dve", lambda e, lo=lo: e.bn_aggr(lnsml[:, lo + 12:lo + 14], lnsml[:, lo:lo + 12]),
               reads=[f"lns{par}", f"lns{par}b"], writes=[f"mv{par}"])
            OP("act", lambda e, lo=lo: e.activation(lnsml[:, lo + 14:lo + 15],
                                                    lnsml[:, lo + 13:lo + 14], AF.Ln,
                                                    bias=1e-5),
               reads=[f"mv{par}", "ceps"], writes=[f"lnv{par}"])
            OP("act", lambda e, lo=lo: e.activation(lnsml[:, lo + 15:lo + 16],
                                                    lnsml[:, lo + 14:lo + 15], AF.Exp,
                                                    scale=-0.5),
               reads=[f"lnv{par}"], writes=[f"rstd{par}"])
            OP("dve", lambda e, xc=xc, lo=lo, xnbf=xnbf: e.tensor_scalar(
                xnbf[:], xc, lnsml[:, lo + 12:lo + 13], lnsml[:, lo + 15:lo + 16],
                ALU.subtract, ALU.mult),
               reads=[xslot, f"mv{par}", f"rstd{par}"], writes=[f"xnbf{par}"])
            for j in range(8):
                OP("pe", lambda e, j=j, xnbf=xnbf: e.transpose(
                    ps_b[:, 1024 + j * 128: 1024 + (j + 1) * 128],
                    xnbf[:, j * 128:(j + 1) * 128], ident[:]),
                    reads=[f"xnbf{par}", "ident"], writes=["bank7"])
            OP("dve", lambda e, c4=c4: e.tensor_copy(
                xntp[:].rearrange("p (i w) -> p i w", i=8)[:, :, c4 * 128:(c4 + 1) * 128],
                ps_b[:, 1024:2048].rearrange("p (i w) -> p i w", i=8)),
                reads=["bank7"], writes=["xntp"])
            OP("dve", lambda e, c4=c4: e.tensor_copy(
                xnst[:].rearrange("p (i w) -> p i w", i=8)[:, :, c4 * 128:(c4 + 1) * 128]
                    .rearrange("p i (g o) -> p i g o", o=4)[:, :, :, 0:1],
                ps_b[:, 1024:2048].rearrange("p (i w) -> p i w", i=8)[:, :, 0:32]
                    .rearrange("p i (g o) -> p i g o", o=1)),
                reads=["bank7"], writes=["xnst"])
            OP("dve", lambda e, c4=c4: e.tensor_copy(
                xnst[:].rearrange("p (i w) -> p i w", i=8)[:, :, c4 * 128:(c4 + 1) * 128]
                    .rearrange("p i (g o) -> p i g o", o=4)[:, :, :, 1:4],
                ps_b[:, 1024:2048].rearrange("p (i w) -> p i w", i=8)[:, :, 32:128]
                    .rearrange("p i (g o) -> p i g o", o=3)),
                reads=["bank7"], writes=["xnst"])
        OP("gp", lambda e, t=t: e.dma_start(
            out=xnT_scr[:, :, t * 512 + 1: (t + 1) * 512 + 1].rearrange("i p w -> p i w"),
            in_=xnst[:].rearrange("p (i w) -> p i w", i=8)),
            reads=["xnst"], dma="xnts")
        for m in range(8):
            pb = f"bank{m % 2}"
            for i in range(8):
                OP("pe", lambda e, m=m, i=i: e.matmul(
                    fb(m % 2), kw_sb[:, i * 1024 + m * 128: i * 1024 + (m + 1) * 128],
                    xntp[:, i * 512:(i + 1) * 512], start=(i == 0), stop=(i == 7)),
                    reads=["kw", "xntp"], writes=[pb])
            OP("dve", lambda e, m=m, pb=pb, t=t: e.tensor_copy(
                kts[:, m * 1024 + t * 128: m * 1024 + (t + 1) * 128]
                    .rearrange("p (s w) -> p s w", s=4),
                fb(m % 2).rearrange("p (s w) -> p s w", s=4)[:, :, 0:32]),
                reads=[f"bank{m % 2}"], writes=["kts"])
            OP("dve", lambda e, m=m, pb=pb: e.tensor_copy(
                ktcst[:, m * 384:(m + 1) * 384].rearrange("p (s w) -> p s w", s=4),
                fb(m % 2).rearrange("p (s w) -> p s w", s=4)[:, :, 32:128]),
                reads=[f"bank{m % 2}"], writes=["ktcst"])
        OP("gp", lambda e, t=t: e.dma_start(
            out=ktc_scr[:, :, t * 384:(t + 1) * 384].rearrange("i p w -> p i w"),
            in_=ktcst[:].rearrange("p (i w) -> p i w", i=8)),
            reads=["ktcst"], dma="ktcs")
        for c4 in range(4):
            cg = 4 * t + c4
            pb = 2 + 2 * (c4 % 2)
            for half in range(2):
                for i in range(8):
                    OP("pe", lambda e, c4=c4, half=half, i=i, pb=pb: e.matmul(
                        ps_f[:, pb * 512 + half * 512: pb * 512 + (half + 1) * 512],
                        xntp[:, i * 512 + c4 * 128: i * 512 + (c4 + 1) * 128],
                        vw_sb[:, i * 1024 + half * 512: i * 1024 + (half + 1) * 512],
                        start=(i == 0), stop=(i == 7)),
                        reads=["vw", "xntp"], writes=[f"bank{pb + half}"])
            OP("dve", lambda e, cg=cg, pb=pb: e.tensor_copy(
                vs[32 * (cg % 4): 32 * (cg % 4) + 32,
                   (cg // 4) * 2048: (cg // 4) * 2048 + 2048]
                    .rearrange("p (h two s) -> p h two s", h=H, two=2)[:, :, 0:1, :],
                ps_f[0:32, pb * 512: pb * 512 + 1024]
                    .rearrange("p (h two s) -> p h two s", two=1, s=64)),
                reads=[f"bank{pb}", f"bank{pb+1}"], writes=["vs"])
            OP("dve", lambda e, v64=v64st[c4 % 2], pb=pb: e.tensor_copy(
                v64[32:64, :], ps_f[32:64, pb * 512: pb * 512 + 1024]),
                reads=[f"bank{pb}", f"bank{pb+1}"], writes=[f"v64st{c4 % 2}"])
            OP("dve", lambda e, v64=v64st[c4 % 2], pb=pb: e.tensor_copy(
                v64[64:128, :], ps_f[64:128, pb * 512: pb * 512 + 1024]),
                reads=[f"bank{pb}", f"bank{pb+1}"], writes=[f"v64st{c4 % 2}"])
            OP("gp", lambda e, cg=cg, v64=v64st[c4 % 2]: e.dma_start(
                out=v64_scr[cg * 96:(cg + 1) * 96, :], in_=v64[32:128, :]),
                reads=[f"v64st{c4 % 2}"], dma=f"v64w{c4 % 2}")

    p.barrier()

    run_b = "b" in stages
    run_c = "c" in stages
    blevel = 4
    for ch in stages:
        if ch.isdigit():
            blevel = int(ch)
    import os as _os
    bandlv = int(_os.environ.get("BANDLV", "4"))
    if run_b:

        OP("gp", lambda e: e.dma_start(out=qw_sb.rearrange("p (i m) -> p i m", i=8),
                                       in_=q_w.rearrange("(i p) m -> p i m", p=128)),
           writes=["qw"], dma="qw")
        OP("gp", lambda e: e.dma_start(out=ow_sb.rearrange("p (i m) -> p i m", i=8),
                                       in_=o_w.rearrange("(i p) m -> p i m", p=128)),
           writes=["ow"], dma="ow")
        OP("dve", lambda e: e.memset(qte[64:128, :], 0.0), writes=["qte"])
        OP("dve", lambda e: e.memset(qto[0:64, :], 0.0), writes=["qto"])
        OP("dve", lambda e: e.memset(bandk[:], 0.0), writes=["bandk"])
        OP("dve", lambda e: e.memset(bandvp[:], 0.0), writes=["bandvp"])
        for sg in range(4):
            OP("dve", lambda e, sg=sg: e.memset(bandvi[sg][:], 0.0), writes=[f"bandvi{sg}"])
            OP("dve", lambda e, sg=sg: e.memset(
                bandvi[sg].rearrange("p (h two s) -> p h two s", h=H, two=2)[:, :, 1, :], 1.0),
                writes=[f"bandvi{sg}"])

        for t in range(NT):
            OP("sp", lambda e, t=t: e.dma_start(
                out=xnt_b[:].rearrange("p (i w) -> p i w", i=8),
                in_=xnT_scr[:, :, t * 512 + 1:(t + 1) * 512 + 1].rearrange("i p w -> p i w")),
                writes=["xnt_b"], dma="xnt_b")
            lo = 384 * t - 6
            s0, s1 = max(lo, 0), min(lo + 416, CK)
            OP("sp", lambda e, lo=lo, s0=s0, s1=s1: e.dma_start(
                out=bandk[:].rearrange("p (i w) -> p i w", i=8)[:, :, s0 - lo: s1 - lo],
                in_=ktc_scr[:, :, s0:s1].rearrange("i p w -> p i w")),
                writes=["bandk"], dma="bandk")
            if t == NT - 1:
                pass
            for m in range(8):
                pb = f"bank{m % 2}"
                for i in range(8):
                    OP("pe", lambda e, m=m, i=i: e.matmul(
                        fb(m % 2), qw_sb[:, i * 1024 + m * 128: i * 1024 + (m + 1) * 128],
                        xnt_b[:, i * 512:(i + 1) * 512], start=(i == 0), stop=(i == 7)),
                        reads=["qw", "xnt_b"], writes=[pb])
                OP("dve", lambda e, m=m: e.tensor_copy(
                    qte[0:64, m * 512:(m + 1) * 512], fb(m % 2)[0:64, :]),
                   reads=[pb], writes=["qte"])
                OP("dve", lambda e, m=m: e.tensor_copy(
                    qto[64:128, m * 512:(m + 1) * 512], fb(m % 2)[64:128, :]),
                   reads=[pb], writes=["qto"])
            for sg in range(4 if blevel >= 2 else 0):
                sglob = 4 * t + sg
                vlo = 96 * sglob - 6
                v0, v1 = max(vlo, 0), min(vlo + 128, CK)
                OP("sp", lambda e, vlo=vlo, v0=v0, v1=v1: e.dma_start(
                    out=bandvp[v0 - vlo: v0 - vlo + (v1 - v0), :], in_=v64_scr[v0:v1, :]),
                    writes=["bandvp"], dma="bandvp")
                OP("dve", lambda e, sg=sg: e.tensor_copy(
                    bandvi[sg].rearrange("p (h two s) -> p h two s", h=H, two=2)[:, :, 0:1, :],
                    bandvp[:].rearrange("p (h two s) -> p h two s", two=1, s=64)),
                    reads=["bandvp"], writes=[f"bandvi{sg}"])
                for halfh in range(2):
                    pb2 = 2 * halfh
                    for hh in range(8 if bandlv >= 2 else 0):
                        h = halfh * 8 + hh
                        qsrc = qte if h % 2 == 0 else qto
                        qname = "qte" if h % 2 == 0 else "qto"
                        OP("pe", lambda e, h=h, hh=hh, sg=sg, pb2=pb2, qsrc=qsrc: e.matmul(
                            ps_f[:, pb2 * 512 + hh * 128: pb2 * 512 + (hh + 1) * 128],
                            bandk[:, (h // 2) * 416 + 96 * sg:
                                  (h // 2) * 416 + 96 * sg + 128],
                            qsrc[:, (h // 2) * 512 + sg * 128:
                                 (h // 2) * 512 + (sg + 1) * 128],
                            start=True, stop=True),
                            reads=["bandk", qname], writes=[f"bank{pb2 + hh // 4}"])
                    if bandlv >= 3:
                        OP("act", lambda e, halfh=halfh, pb2=pb2: e.activation(
                            bandp[:, halfh * 1024:(halfh + 1) * 1024],
                            ps_f[:, pb2 * 512: pb2 * 512 + 1024], AF.Exp,
                            scale=qscale[:, 0:1]),
                            reads=[f"bank{pb2}", f"bank{pb2 + 1}", "qscale"],
                            writes=["bandp"])
                if bandlv < 4:
                    pass
                elif sglob == 0 or sglob == NT * 4 - 1:
                    mk = maskf if sglob == 0 else maskl
                    mname = "maskf" if sglob == 0 else "maskl"
                    for h in range(H):
                        OP("dve", lambda e, sg=sg, h=h, mk=mk: e.tensor_mul(
                            bandpm[sg][:, h * 128:(h + 1) * 128],
                            bandp[:, h * 128:(h + 1) * 128], mk[:]),
                            reads=["bandp", mname], writes=[f"bandpm{sg}"])
                else:
                    OP("dve", lambda e, sg=sg: e.tensor_mul(bandpm[sg][:], bandp[:], maskr[:]),
                       reads=["bandp", "maskr"], writes=[f"bandpm{sg}"])
            for pr in range(8 if blevel >= 3 else 0):
                he, ho = 2 * pr, 2 * pr + 1
                def emit_scores(g):
                    pb2 = 2 * (g % 2)
                    for (par, tp_c, kk) in ((0, 0, 2 * g), (0, 64, 2 * g + 1),
                                            (1, 0, 2 * g), (1, 64, 2 * g + 1)):
                        qsrc = qte if par == 0 else qto
                        qname = "qte" if par == 0 else "qto"
                        OP("pe", lambda e, par=par, tp_c=tp_c, kk=kk, pr=pr, pb2=pb2,
                            qsrc=qsrc: e.matmul(
                            ps_f[tp_c:tp_c + 64, pb2 * 512 + par * 512:
                                 pb2 * 512 + par * 512 + 512],
                            kts[:, pr * 1024 + kk * 64: pr * 1024 + (kk + 1) * 64],
                            qsrc[:, pr * 512:(pr + 1) * 512],
                            start=True, stop=True, tile_position=(0, tp_c)),
                            reads=["kts", qname], writes=[f"bank{pb2 + par}"])
                    OP("act", lambda e, g=g, pb2=pb2: e.activation(
                        ptil[g % 3][:], ps_f[:, pb2 * 512: pb2 * 512 + 1024], AF.Exp,
                        scale=qscale[:, 0:1]),
                        reads=[f"bank{pb2}", f"bank{pb2 + 1}", "qscale"],
                        writes=[f"ptil{g % 3}"])

                def emit_av(g):
                    for sub, col in ((0, 0), (1, 512)):
                        OP("pe", lambda e, g=g, pr=pr, sub=sub, col=col: e.matmul(
                            fb(4 + sub), vs[:, g * 2048 + (2 * pr + sub) * 128:
                                            g * 2048 + (2 * pr + sub) * 128 + 128],
                            ptil[g % 3][:, col:col + 512],
                            start=(g == 0), stop=False),
                            reads=[f"ptil{g % 3}", "vs"], writes=[f"bank{4 + sub}"])

                emit_scores(0)
                emit_scores(1)
                for g in range(2, 8):
                    emit_scores(g)
                    emit_av(g - 2)
                emit_av(6)
                emit_av(7)
                for sg in range(4 if blevel >= 2 else 0):
                    for sub in range(2):
                        h = 2 * pr + sub
                        OP("pe", lambda e, sg=sg, h=h, sub=sub, last=(sg == 3): e.matmul(
                            ps_f[:, (4 + sub) * 512 + sg * 128: (4 + sub) * 512 + (sg + 1) * 128],
                            bandvi[sg][:, h * 128:(h + 1) * 128],
                            bandpm[sg][:, h * 128:(h + 1) * 128],
                            start=False, stop=last),
                            reads=[f"bandvi{sg}", f"bandpm{sg}"], writes=[f"bank{4 + sub}"])
                OP("act", lambda e: e.activation(lnb[0:64, :], ps_f[64:128, 2048:3072], AF.Ln),
                   reads=["bank4", "bank5"], writes=["lnb"])
                OP("act", lambda e: e.activation(recb[0:64, :], lnb[0:64, :], AF.Exp, scale=-1.0),
                   reads=["lnb"], writes=["recb"])
                OP("dve", lambda e, pr=pr: e.tensor_mul(
                    attnt[0:64, pr * 512:(pr + 1) * 512], ps_f[0:64, 2048:2560],
                    recb[0:64, 0:512]),
                    reads=["bank4", "recb"], writes=["attnt"])
                OP("dve", lambda e, pr=pr: e.tensor_mul(
                    attnt[64:128, pr * 512:(pr + 1) * 512], ps_f[0:64, 2560:3072],
                    recb[0:64, 512:1024]),
                    reads=["bank5", "recb"], writes=["attnt"])
            for sub in range(4 if blevel >= 4 and blevel >= 3 else 0):
                for half in range(2):
                    pb = half
                    pname = f"bank{half}"
                    OP("pe", lambda e, half=half, pb=pb: e.matmul(
                        fb(pb), biasst[:], obrep[:, half * 512:(half + 1) * 512],
                        start=True, stop=False),
                        reads=["biasst", "obrep"], writes=[pname])
                    for i in range(8):
                        OP("pe", lambda e, sub=sub, half=half, i=i, pb=pb: e.matmul(
                            fb(pb), attnt[:, i * 512 + sub * 128: i * 512 + sub * 128 + 128],
                            ow_sb[:, i * 1024 + half * 512: i * 1024 + (half + 1) * 512],
                            start=False, stop=(i == 7)),
                            reads=["attnt", "ow"], writes=[pname])
                    OP("dve", lambda e, half=half, pb=pb: e.tensor_copy(
                        outst[:, half * 512:(half + 1) * 512], fb(pb)),
                        reads=[pname], writes=[f"outst{half}"])
                    OP("gp", lambda e, t=t, sub=sub, half=half: e.dma_start(
                        out=out[t * 512 + sub * 128: t * 512 + (sub + 1) * 128,
                                half * 512:(half + 1) * 512],
                        in_=outst[:, half * 512:(half + 1) * 512], accum_op=ALU.add),
                        reads=[f"outst{half}"], writes=[f"outr{t}"], dma=f"oacc{half}")

    p.barrier()

    if run_c:
        for wi, dst in enumerate((wrt, wkt, wvt)):
            OP("sp", lambda e, wi=wi, dst=dst: e.dma_start(
                out=dst.rearrange("p (i m) -> p i m", i=8),
                in_=wt_scr[wi].rearrange("p (i m) -> p i m", i=8)),
                writes=[f"wt{wi}"], dma=f"wt{wi}")
        for t in range(NT):
            OP("sp", lambda e, t=t: e.dma_start(
                out=xnt_c[:].rearrange("p (i w) -> p i w", i=8),
                in_=xnT_scr[:, :, t * 512:(t + 1) * 512 + 1].rearrange("i p w -> p i w")),
                writes=["xnt_c"], dma="xnt_c")
            for i in range(8):
                cur = xnt_c[:, i * 513 + 1: i * 513 + 513]
                prev = xnt_c[:, i * 513: i * 513 + 512]
                OP("dve", lambda e, cur=cur, prev=prev: e.tensor_sub(dtmp[:], cur, prev),
                   reads=["xnt_c"], writes=["dtmp"])
                OP("dve", lambda e, i=i, prev=prev: e.scalar_tensor_tensor(
                    xr_c[:, i * 512:(i + 1) * 512], dtmp[:], muscal[:, i:i + 1], prev,
                    ALU.mult, ALU.add),
                    reads=["dtmp", "mu", "xnt_c"], writes=["xr"])
                OP("dve", lambda e, i=i, prev=prev: e.scalar_tensor_tensor(
                    xk_c[:, i * 512:(i + 1) * 512], dtmp[:], muscal[:, 8 + i:9 + i], prev,
                    ALU.mult, ALU.add),
                    reads=["dtmp", "mu2", "xnt_c"], writes=["xk"])
            for m in range(8):
                pname = f"bank{m % 2}"
                for i in range(8):
                    OP("pe", lambda e, m=m, i=i: e.matmul(
                        fb(m % 2), wkt[:, i * 1024 + m * 128: i * 1024 + (m + 1) * 128],
                        xk_c[:, i * 512:(i + 1) * 512], start=(i == 0), stop=(i == 7)),
                        reads=["wt1", "xk"], writes=[pname])
                OP("dve", lambda e, m=m: e.tensor_scalar(krel[:], fb(m % 2), 0.0, None, ALU.max),
                   reads=[pname], writes=["krel"])
                OP("dve", lambda e, m=m: e.tensor_mul(kact[:, m * 512:(m + 1) * 512],
                                                      krel[:], krel[:]),
                   reads=["krel"], writes=["kact"])
            for sub in range(4):
                for half in range(2):
                    pr_ = f"bank{2 + half}"
                    pv_ = f"bank{4 + half}"
                    for i in range(8):
                        OP("pe", lambda e, sub=sub, half=half, i=i: e.matmul(
                            fb(2 + half), xr_c[:, i * 512 + sub * 128: i * 512 + sub * 128 + 128],
                            wrt[:, i * 1024 + half * 512: i * 1024 + (half + 1) * 512],
                            start=(i == 0), stop=(i == 7)),
                            reads=["wt0", "xr"], writes=[pr_])
                    OP("act", lambda e, half=half: e.activation(rt_c[:], fb(2 + half), AF.Tanh,
                                                                scale=0.5),
                       reads=[pr_], writes=["rt"])
                    for i in range(8):
                        OP("pe", lambda e, sub=sub, half=half, i=i: e.matmul(
                            fb(4 + half), kact[:, i * 512 + sub * 128: i * 512 + sub * 128 + 128],
                            wvt[:, i * 1024 + half * 512: i * 1024 + (half + 1) * 512],
                            start=(i == 0), stop=(i == 7)),
                            reads=["wt2", "kact"], writes=[pv_])
                    OP("dve", lambda e, half=half: e.scalar_tensor_tensor(
                        outc[:, half * 512:(half + 1) * 512], rt_c[:], 1.0, fb(4 + half),
                        ALU.add, ALU.mult),
                        reads=["rt", pv_], writes=[f"outc{half}"])
                    OP("gp", lambda e, t=t, sub=sub, half=half: e.dma_start(
                        out=out[t * 512 + sub * 128: t * 512 + (sub + 1) * 128,
                                half * 512:(half + 1) * 512],
                        in_=outc[:, half * 512:(half + 1) * 512], accum_op=ALU.add),
                        reads=[f"outc{half}"], writes=[f"outr{t}"], dma=f"oaccC{half}")

    slots = [f"pre{par}_{c4}" for par in range(2) for c4 in range(4)]
    if run_b and blevel >= 4:
        slots += ["oacc0", "oacc1"]
    if run_c:
        slots += ["oaccC0", "oaccC1"]
    p.final_wait("gp", slots)

    p.prepare()
    with nc.Block() as block:
        @block.sync
        def _(eng):
            p.emit_engine("sp", eng)

        @block.gpsimd
        def _(eng):
            p.emit_engine("gp", eng)

        @block.vector
        def _(eng):
            p.emit_engine("dve", eng)

        @block.scalar
        def _(eng):
            p.emit_engine("act", eng)

        @block.tensor
        def _(eng):
            p.emit_engine("pe", eng)

    es.close()
    return nc


def make_inputs(inp):
    f32 = np.float32
    maskr, maskf, maskl = _band_masks()
    d = {
        "q_w": np.ascontiguousarray(inp["L0_attn_q_w"], f32),
        "k_w": np.ascontiguousarray(inp["L0_attn_k_w"], f32),
        "v_w": np.ascontiguousarray(inp["L0_attn_v_w"], f32),
        "o_w": np.ascontiguousarray(inp["L0_attn_o_w"], f32),
        "o_b": np.ascontiguousarray(inp["L0_attn_o_b"], f32).reshape(1, D),
        "us": np.asarray(inp["L0_attn_unity_scale"], f32).reshape(1, 1),
        "w_r": np.ascontiguousarray(inp["L0_ffn_W_r"], f32),
        "w_k": np.ascontiguousarray(inp["L0_ffn_W_k"], f32),
        "w_v": np.ascontiguousarray(inp["L0_ffn_W_v"], f32),
        "mu_r": np.ascontiguousarray(inp["L0_ffn_mu_r"], f32),
        "mu_k": np.ascontiguousarray(inp["L0_ffn_mu_k"], f32),
        "ident": np.eye(128, dtype=ml_dtypes.bfloat16),
        "maskr": maskr, "maskf": maskf, "maskl": maskl,
    }
    return d


_NC_CACHE = [None]


def kernel(**inputs):
    import os
    if _NC_CACHE[0] is None:
        _NC_CACHE[0] = build_nc(os.environ.get("KSTAGES", "abc"))
    nc = _NC_CACHE[0]
    base = make_inputs(inputs)
    xfull = np.ascontiguousarray(inputs["L0_attn_x"], np.float32)
    in_maps = []
    for c in range(NC):
        m = dict(base)
        m["x"] = xfull[c]
        in_maps.append(m)
    res = run_bass_kernel_spmd(nc, in_maps, core_ids=list(range(NC)))
    return np.stack([r["out"] for r in res.results], axis=0)


if __name__ == "__main__":
    rng = np.random.default_rng(0)
    fake = {
        "L0_attn_x": rng.standard_normal((NC, S, D), dtype=np.float32),
        "L0_attn_q_w": rng.standard_normal((D, D), dtype=np.float32) / 32,
        "L0_attn_k_w": rng.standard_normal((D, D), dtype=np.float32) / 32,
        "L0_attn_v_w": rng.standard_normal((D, D), dtype=np.float32) / 32,
        "L0_attn_o_w": rng.standard_normal((D, D), dtype=np.float32) / 32,
        "L0_attn_o_b": rng.standard_normal(D).astype(np.float32) * 0.01,
        "L0_attn_unity_scale": np.float32(1.618),
        "L0_ffn_W_r": (rng.random((D, D), dtype=np.float32) - 0.5) / 16,
        "L0_ffn_W_k": (rng.random((D, D), dtype=np.float32) - 0.5) / 16,
        "L0_ffn_W_v": (rng.random((D, D), dtype=np.float32) - 0.5) / 16,
        "L0_ffn_mu_r": rng.random(D, dtype=np.float32),
        "L0_ffn_mu_k": rng.random(D, dtype=np.float32),
    }
    out = kernel(**fake)
    print("ran", out.shape, out.dtype)
